# version 47
# baseline (speedup 1.0000x reference)
import contextlib

import numpy as np

import concourse.bass as bass
import concourse.mybir as mybir
from concourse.bass_utils import run_bass_kernel_spmd
from concourse import dve_ops

F32 = mybir.dt.float32
BF16 = mybir.dt.bfloat16
AF = mybir.ActivationFunctionType
ALU = mybir.AluOpType
AX = mybir.AxisListType

D = 1024
DC = 8
D2 = 4096
DI = 2048
EC = 16
S = 16
TT = 512
NT = 4
W = 256
TC = W + NT * TT
NCORES = 8
EPS = 1e-5


def build_program():
    nc = bass.Bass("TRN2", target_bir_lowering=False)

    d_xT = nc.dram_tensor("xT", [D, TC], F32, kind="ExternalInput").ap()
    d_wfT = nc.dram_tensor("wfT", [D, D], F32, kind="ExternalInput").ap()
    d_winT = nc.dram_tensor("winT", [D, D2], F32, kind="ExternalInput").ap()
    d_wbT = nc.dram_tensor("wbT", [DI, S], F32, kind="ExternalInput").ap()
    d_wcT = nc.dram_tensor("wcT", [S, DI], F32, kind="ExternalInput").ap()
    d_woT = nc.dram_tensor("woT", [DI, D], F32, kind="ExternalInput").ap()
    d_bin = nc.dram_tensor("b_in_pc", [128, D2 // 128], F32, kind="ExternalInput").ap()
    d_bout = nc.dram_tensor("b_out_pc", [128, DC], F32, kind="ExternalInput").ap()
    d_dvec = nc.dram_tensor("d_pc", [128, EC], F32, kind="ExternalInput").ap()
    d_mask = nc.dram_tensor("mask_w", [S, W], F32, kind="ExternalInput").ap()
    d_outT = nc.dram_tensor("outT", [D, NT * TT], F32, kind="ExternalOutput").ap()

    es = contextlib.ExitStack()
    sb = lambda name, shape, dt: es.enter_context(nc.sbuf_tensor(name, shape, dt))
    psum = lambda name, shape, dt: es.enter_context(nc.psum_tensor(name, shape, dt))
    sem = lambda name: es.enter_context(nc.semaphore(name))

    xb = sb("xb", [128, 2, DC, TT], F32)
    winT = sb("winT_bf", [128, DC, D2], BF16)
    woT = sb("woT_bf", [128, EC, D], BF16)
    wfT = sb("wfT_bf", [128, DC, D], BF16)
    wbT = sb("wbT_bf", [128, EC, S], BF16)
    wcT = sb("wcT_bf", [S, DI], BF16)
    xz = sb("xz_bf", [128, 2, EC, TT], BF16)
    lnbf = sb("lnbf", [128, DC, TT], BF16)
    scr = sb("scr", [128, 1, TT], F32)
    rows = sb("rows", [1, 2, TT], F32)
    rows_bf = sb("rows_bf", [1, 2, TT], BF16)
    rbm_bf = sb("rbm_bf", [128, 2, TT], BF16)
    constz = sb("constz", [128, 1], F32)
    b_in = sb("b_in", [128, D2 // 128], F32)
    b_out = sb("b_out", [128, DC], F32)
    dvec = sb("dvec", [128, EC], F32)
    maskw = sb("maskw", [S, W], BF16)
    bsc = sb("bsc", [S, TT], F32)
    hsbf = sb("hs_bf", [S, TT], BF16)
    carry = sb("carry", [S, 1], F32)
    c09 = sb("c09", [S, 1], F32)
    c01 = sb("c01", [S, 1], F32)
    onesr_bf = sb("onesr_bf", [1, 128], BF16)
    onesr = sb("onesr", [1, 128], F32)
    onesc = sb("onesc", [128, 1], BF16)
    onescf = sb("onescf", [128, 1], F32)
    epsap = sb("epsap", [1, 1], F32)
    absc = sb("absc", [128, 2], F32)
    sc05 = sb("sc05", [128, 1], F32)
    rsc = sb("rsc", [128, 1], F32)
    srow = sb("srow", [1, 8], F32)

    pg = psum("pg", [128, 8, TT], F32)

    sqbf = sb("sqbf", [128, 4, TT], BF16)
    xpbf = xz[:, 0, :, :]
    zsbf = xz[:, 1, :, :]
    lnA = xz[:, 0, 0:DC, :]
    wfA = xb[:, 1, :, :].rearrange("p c t -> p (c t)")
    wfB = xb[:, 0, :, 256:512]
    wfC = xb[:, 0, :, 0:256]

    s_pe = sem("s_pe")
    s_v = sem("s_v")
    s_sc = sem("s_sc")
    d_s = sem("d_s")
    d_wg = sem("d_wg")
    d_mk = sem("d_mk")
    d_wo = sem("d_wo")
    d_wi = [sem(f"d_wi{j}") for j in range(8)]
    d_out = sem("d_out")
    d_xt = [sem(f"d_x{t}") for t in range(NT + 1)]

    C = {"pe": 0, "v": 0, "sc": 0, "dout": 0}
    SEMS = {"pe": None, "v": None, "sc": None}

    blk = es.enter_context(nc.Block())
    pe, ve, sc, gp, sy = blk.tensor, blk.vector, blk.scalar, blk.gpsimd, blk.sync

    def _sem(kind):
        return {"pe": s_pe, "v": s_v, "sc": s_sc}[kind]

    scq_items = []

    def scq_add(waits, fn):
        C["sc"] += 1
        scq_items.append((list(waits), fn))
        return C["sc"]

    def scq_flush():
        if not scq_items:
            return
        items = list(scq_items)
        del scq_items[:]

        def _sec(s):
            for waits, fn in items:
                for kind, cnt in waits:
                    s.wait_ge(_sem(kind) if isinstance(kind, str) else kind, cnt)
                fn(s).then_inc(s_sc, 1)

        sc(_sec)

    bank_free = {b: ("v", 0) for b in range(8)}

    def wait_banks(eng, banks):
        need = {}
        for b in banks:
            kind, cnt = bank_free[b]
            if cnt > 0:
                need[kind] = max(need.get(kind, 0), cnt)
        for kind, cnt in need.items():
            eng.wait_ge(_sem(kind), cnt)

    def _loads(sync):
        for c in range(4):
            sync.dma_start(out=xb[:, 1, 2 * c : 2 * c + 2, :],
                           in_=d_wfT[128 * c : 128 * c + 128, :].rearrange(
                               "p (a t) -> p a t", t=512)).then_inc(d_s, 16)

    sy(_loads)

    def _loads2(s):
        for c in range(2):
            s.dma_start(out=xb[:, 0, 4 * c : 4 * c + 4, 0:256],
                        in_=d_wfT[768 + 128 * c : 896 + 128 * c, :].rearrange(
                            "p (a t) -> p a t", t=256)).then_inc(d_s, 16)
        for c in range(2):
            s.dma_start(out=xb[:, 0, 4 * c : 4 * c + 4, 256:512],
                        in_=d_wfT[512 + 128 * c : 640 + 128 * c, :].rearrange(
                            "p (a t) -> p a t", t=256)).then_inc(d_s, 16)

    sc(_loads2)

    def _casts(g):
        g.dma_start(out=maskw[:], in_=d_mask).then_inc(d_mk, 16)
        g.dma_start(out=b_in[:], in_=d_bin).then_inc(d_wg, 16)
        g.dma_start(out=b_out[:], in_=d_bout).then_inc(d_wg, 16)
        g.dma_start(out=dvec[:], in_=d_dvec).then_inc(d_wg, 16)
        g.dma_start(out=wbT[:], in_=d_wbT.rearrange("(c p) s -> p c s", p=128)).then_inc(d_wg, 16)
        g.dma_start(out=wcT[:], in_=d_wcT).then_inc(d_wg, 16)
        g.wait_ge(d_s, 8 * 16)
        for j in range(8):
            if j == 2:
                g.wait_ge(d_xt[1], 16)
            g.dma_start(out=winT[:, :, 512 * j : 512 * (j + 1)],
                        in_=d_winT.rearrange("(c p) o -> p c o", p=128)[:, :, 512 * j : 512 * (j + 1)]
                        ).then_inc(d_wi[j], 16)
        g.dma_start(out=woT[:], in_=d_woT.rearrange("(c p) o -> p c o", p=128)).then_inc(d_wo, 16)

    gp(_casts)

    def _consts(v):
        v.memset(onesr[:], 1.0)
        v.memset(onesr_bf[:], 1.0)
        v.memset(onesc[:], 1.0)
        v.memset(onescf[:], 1.0)
        v.memset(epsap[:], EPS)
        v.memset(constz[:], 0.0)
        v.memset(carry[:], 0.0)
        v.memset(c09[:], 9.0)
        v.memset(absc[:], 0.0)
        v.memset(c01[:], 10.0).then_inc(s_v, 1)

    ve(_consts)
    C["v"] += 1

    def _cdiv(v):
        v.wait_ge(s_v, C["v"])
        v.reciprocal(out=c01[:], in_=c01[:]).then_inc(s_v, 1)
        v.wait_ge(s_v, C["v"] + 1)
        v.tensor_scalar(out=c09[:], in0=c01[:], scalar1=9.0, scalar2=None, op0=ALU.mult).then_inc(s_v, 1)

    ve(_cdiv)
    C["v"] += 2

    zq = xz[:, 1, :, :].rearrange("p c t -> p (c t)")

    wfBC = xb[:, 0, :, :].rearrange("p c t -> p (c t)")

    def _abssum(s, vw=C["v"]):
        s.wait_ge(s_v, vw)
        s.wait_ge(d_s, 8 * 16)
        s.activation(out=zq[:, 0:4096], in_=wfA, func=AF.Abs, bias=0.0,
                     scale=1.0, accum_out=absc[:, 0:1])
        s.activation(out=zq[:, 4096:8192], in_=wfBC, func=AF.Abs, bias=0.0,
                     scale=1.0, accum_out=absc[:, 1:2]).then_inc(s_sc, 1)

    sc(_abssum)
    C["sc"] += 1

    def _absmm(t):
        t.wait_ge(s_sc, C["sc"])
        t.wait_ge(s_v, C["v"])
        t.matmul(pg[0:1, 5, 0:2], onescf[:, 0:1], absc[:, 0:2], start=True, stop=True).then_inc(s_pe, 1)

    pe(_absmm)
    C["pe"] += 1

    def _scale(v):
        v.wait_ge(s_pe, C["pe"])
        v.tensor_copy(out=srow[0:1, 0:2], in_=pg[0:1, 5, 0:2]).then_inc(s_v, 1)
        v.wait_ge(s_v, C["v"] + 1)
        v.tensor_tensor(out=srow[0:1, 0:1], in0=srow[0:1, 0:1], in1=srow[0:1, 1:2], op=ALU.add).then_inc(s_v, 1)
        v.wait_ge(s_v, C["v"] + 2)
        v.tensor_scalar(out=srow[0:1, 2:3], in0=srow[0:1, 0:1], scalar1=1.0 / (D * D), scalar2=None, op0=ALU.mult).then_inc(s_v, 1)
        v.wait_ge(s_v, C["v"] + 3)
        v.tensor_scalar(out=srow[0:1, 3:4], in0=srow[0:1, 2:3], scalar1=EPS, scalar2=0.5, op0=ALU.max, op1=ALU.mult).then_inc(s_v, 1)
        v.wait_ge(s_v, C["v"] + 4)
        v.tensor_scalar(out=srow[0:1, 4:5], in0=srow[0:1, 3:4], scalar1=-1.0, scalar2=None, op0=ALU.mult).then_inc(s_v, 1)

    ve(_scale)
    C["v"] += 5

    def _scalebc(t):
        t.wait_ge(s_v, C["v"])
        t.matmul(pg[:, 4, 0:1], onesr[0:1, :], srow[0:1, 3:4], start=True, stop=True)
        mm = t.matmul(pg[:, 4, 1:2], onesr[0:1, :], srow[0:1, 4:5], start=True, stop=True)
        mm.then_inc(s_pe, 1)

    pe(_scalebc)
    C["pe"] += 1

    def _scalecp(v):
        v.wait_ge(s_pe, C["pe"])
        v.tensor_copy(out=sc05[:], in_=pg[:, 4, 0:1])
        v.tensor_copy(out=rsc[:], in_=pg[:, 4, 1:2]).then_inc(s_v, 1)

    ve(_scalecp)
    C["v"] += 1
    bank_free[4] = ("v", C["v"])
    bank_free[5] = ("v", C["v"])

    wfT_flat = wfT[:, :, :].rearrange("p c o -> p (c o)")
    QV = {}

    def _quant(v):
        n = 0
        for name, qsrc, lo, hi in (("C", wfC, 6144, 8192), ("A", wfA, 0, 4096),
                                   ("B", wfB, 4096, 6144)):
            dst = wfT_flat[:, lo:hi]
            tmp = zq[:, 0 : hi - lo]
            if name in ("B", "C"):
                dst = dst.rearrange("p (a t) -> p a t", t=256)
                tmp = tmp.rearrange("p (a t) -> p a t", t=256)
            v.wait_ge(s_v, C["v"] + n)
            v.tensor_scalar(out=tmp, in0=qsrc, scalar1=rsc[:, 0:1], scalar2=None, op0=ALU.is_le).then_inc(s_v, 1)
            v.tensor_scalar(out=dst, in0=qsrc, scalar1=sc05[:, 0:1], scalar2=None, op0=ALU.is_ge).then_inc(s_v, 1)
            QV[name] = C["v"] + n + 2
            v.wait_ge(s_v, C["v"] + n + 2)
            v.tensor_tensor(out=dst, in0=dst, in1=tmp, op=ALU.subtract).then_inc(s_v, 1)
            n += 3

    def x_of(tile):
        tw = W if tile == 0 else TT
        return xb[:, tile % 2, :, 0:tw], tw

    rowA = rows[0:1, 0, :]
    rowB = rows[0:1, 1, :]
    rowS = scr[0:1, 0, :]

    def ln_cast(tile, buf, deps=(), vdep=None):
        xsrc, tw = x_of(tile)
        vw = [] if vdep == 0 else [("v", C["v"] if vdep is None else vdep)]
        scq_add(vw + list(deps),
                lambda s, buf=buf, xsrc=xsrc, tw=tw: s.activation(
                    out=buf[:, :, 0:tw], in_=xsrc, func=AF.Copy, bias=0.0, scale=1.0))

    def ln_summ(tile, buf, bank):
        _, tw = x_of(tile)

        def _f(t, sc_cnt=C["sc"]):
            t.wait_ge(s_sc, sc_cnt)
            wait_banks(t, [bank])
            for kc in range(DC):
                mm = t.matmul(pg[0:1, bank, 0:tw], onesc[:, 0:1], buf[:, kc, 0:tw],
                              start=(kc == 0), stop=(kc == DC - 1))
            mm.then_inc(s_pe, 1)

        pe(_f)
        C["pe"] += 1

    def ln_sumsq(tile, buf, bank):
        _, tw = x_of(tile)
        for q in range(2):
            cnt = scq_add([("pe", C["pe"])],
                          lambda s, q=q, tw=tw, buf=buf: s.activation(
                              out=sqbf[:, :, 0:tw], in_=buf[:, 4 * q : 4 * q + 4, 0:tw],
                              func=AF.Square, bias=0.0, scale=1.0))

            def _mm(t, q=q, c=cnt):
                t.wait_ge(s_sc, c)
                if q == 0:
                    wait_banks(t, [bank])
                for j in range(4):
                    mm = t.matmul(pg[0:1, bank, 0:tw], onesc[:, 0:1], sqbf[:, j, 0:tw],
                                  start=(q == 0 and j == 0), stop=(q == 1 and j == 3))
                mm.then_inc(s_pe, 1)

            pe(_mm)
            C["pe"] += 1

    def ln_rows(tile, banks):
        _, tw = x_of(tile)
        b0, b1 = banks

        def _f(v):
            v.wait_ge(s_pe, C["pe"])
            v.tensor_scalar(out=rows[0:1, :, 0:tw], in0=pg[0:1, b0 : b0 + 2, 0:tw],
                            scalar1=1.0 / D, scalar2=None, op0=ALU.mult).then_inc(s_v, 1)
            v.wait_ge(s_v, C["v"] + 1)
            v.tensor_tensor(out=rowS[:, 0:tw], in0=rowA[:, 0:tw], in1=rowA[:, 0:tw], op=ALU.mult).then_inc(s_v, 1)
            v.wait_ge(s_v, C["v"] + 2)
            v.tensor_tensor(out=rowB[:, 0:tw], in0=rowB[:, 0:tw], in1=rowS[:, 0:tw], op=ALU.subtract).then_inc(s_v, 1)

        ve(_f)
        C["v"] += 3

    def ln_sqrt(tile):
        _, tw = x_of(tile)
        scq_add([("v", C["v"])],
                lambda s, tw=tw: s.activation(out=rowB[:, 0:tw], in_=rowB[:, 0:tw],
                                              func=AF.Ln, bias=epsap[0:1, 0:1], scale=1.0))
        scq_add([("sc", C["sc"])],
                lambda s, tw=tw: s.activation(out=rowB[:, 0:tw], in_=rowB[:, 0:tw],
                                              func=AF.Exp, bias=0.0, scale=-0.5))

    def ln_fin(tile):
        _, tw = x_of(tile)

        def _f(v):
            v.wait_ge(s_sc, C["sc"])
            v.tensor_tensor(out=rowA[:, 0:tw], in0=rowA[:, 0:tw], in1=rowB[:, 0:tw], op=ALU.mult).then_inc(s_v, 1)
            v.wait_ge(s_v, C["v"] + 1)
            v.tensor_copy(out=rows_bf[0:1, :, 0:tw], in_=rows[0:1, :, 0:tw]).then_inc(s_v, 1)

        ve(_f)
        C["v"] += 2

    def ln_bcast(tile, banks, v_cnt=None):
        _, tw = x_of(tile)
        b0, b1 = banks
        vw = C["v"] if v_cnt is None else v_cnt

        def _f(t):
            t.wait_ge(s_v, vw)
            wait_banks(t, [b0, b1])
            t.matmul(pg[:, b0, 0:tw], onesr_bf[0:1, :], rows_bf[0:1, 1, 0:tw], start=True, stop=True)
            mm = t.matmul(pg[:, b1, 0:tw], onesr_bf[0:1, :], rows_bf[0:1, 0, 0:tw], start=True, stop=True)
            mm.then_inc(s_pe, 1)

        pe(_f)
        C["pe"] += 1

    def ln_copyrb(tile, banks):
        _, tw = x_of(tile)
        b0, b1 = banks
        scq_add([("pe", C["pe"])],
                lambda s, tw=tw, b0=b0: s.activation(out=rbm_bf[:, 0, 0:tw], in_=pg[:, b0, 0:tw],
                                                     func=AF.Copy, bias=0.0, scale=1.0))
        cnt = scq_add([],
                      lambda s, tw=tw, b1=b1: s.activation(out=rbm_bf[:, 1, 0:tw], in_=pg[:, b1, 0:tw],
                                                           func=AF.Copy, bias=0.0, scale=1.0))
        bank_free[b0] = ("sc", cnt)
        bank_free[b1] = ("sc", cnt)

    def ln_apply(tile, buf, banks):
        _, tw = x_of(tile)
        base = C["v"]

        def _f(v):
            v.wait_ge(s_sc, C["sc"])
            v.tensor_tensor(out=buf[:, :, 0:tw], in0=buf[:, :, 0:tw],
                            in1=rbm_bf[:, 0:1, 0:tw].broadcast_to([128, DC, tw]),
                            op=ALU.mult).then_inc(s_v, 1)
            v.wait_ge(s_v, base + 1)
            v.tensor_tensor(out=buf[:, :, 0:tw], in0=buf[:, :, 0:tw],
                            in1=rbm_bf[:, 1:2, 0:tw].broadcast_to([128, DC, tw]),
                            op=ALU.subtract).then_inc(s_v, 1)

        ve(_f)
        C["v"] += 2

    def ffn_round(tile, buf, m, bank, v_ready):
        xsrc, tw = x_of(tile)

        def _mm(t):
            t.wait_ge(s_v, v_ready)
            wait_banks(t, [bank])
            for kc in range(DC):
                mm = t.matmul(pg[:, bank, 0:tw], wfT[:, kc, m * 128 : (m + 1) * 128],
                              buf[:, kc, 0:tw], start=(kc == 0), stop=(kc == DC - 1))
            mm.then_inc(s_pe, 1)

        pe(_mm)
        C["pe"] += 1

        def _evac(v):
            v.wait_ge(s_pe, C["pe"])
            v.scalar_tensor_tensor(out=xsrc[:, m, :], in0=pg[:, bank, 0:tw],
                                   scalar=sc05[:, 0:1], in1=xsrc[:, m, :],
                                   op0=ALU.mult, op1=ALU.add).then_inc(s_v, 1)

        ve(_evac)
        C["v"] += 1
        bank_free[bank] = ("v", C["v"])

    def win_round(tile, buf, m, bank, v_ready, warm):
        _, tw = x_of(tile)

        def _mm(t):
            t.wait_ge(d_wi[m // 4], 16)
            t.wait_ge(s_v, v_ready)
            wait_banks(t, [bank])
            for kc in range(DC):
                mm = t.matmul(pg[:, bank, 0:tw], winT[:, kc, m * 128 : (m + 1) * 128],
                              buf[:, kc, 0:tw], start=(kc == 0), stop=(kc == DC - 1))
            mm.then_inc(s_pe, 1)

        pe(_mm)
        C["pe"] += 1

        if warm:
            out = zsbf[:, m, 0:tw] if m < DC else xpbf[:, m, 0:tw]
        else:
            out = xpbf[:, m, 0:tw] if m < EC else zsbf[:, m - EC, 0:tw]
        cnt = scq_add([("pe", C["pe"])],
                      lambda s, out=out, bank=bank, m=m, tw=tw: s.activation(
                          out=out, in_=pg[:, bank, 0:tw], func=AF.Silu,
                          bias=b_in[:, m : m + 1], scale=1.0))
        bank_free[bank] = ("sc", cnt)

    def wb_mm(tile, bank, warm, head_sc=None):
        _, tw = x_of(tile)

        def _rhs(kc):
            if warm:
                return zsbf[:, kc, 0:tw] if kc < DC else xpbf[:, kc, 0:tw]
            return xpbf[:, kc, 0:tw]

        def _head(t, hs=head_sc if head_sc is not None else C["sc"]):
            t.wait_ge(s_sc, hs)
            wait_banks(t, [bank])
            for kc in range(EC - 2):
                t.matmul(pg[0:16, bank, 0:tw], wbT[:, kc, :], _rhs(kc),
                         start=(kc == 0), stop=False)

        pe(_head)

        def _tail(t):
            t.wait_ge(s_sc, C["sc"])
            for kc in range(EC - 2, EC):
                mm = t.matmul(pg[0:16, bank, 0:tw], wbT[:, kc, :], _rhs(kc),
                              start=False, stop=(kc == EC - 1))
            mm.then_inc(s_pe, 1)

        pe(_tail)
        C["pe"] += 1

    def scan_stage(tile, bank, warm):
        _, tw = x_of(tile)
        base = C["v"]

        def _f(v):
            v.wait_ge(s_pe, C["pe"])
            if warm:
                v.wait_ge(d_mk, 16)
                v.scalar_tensor_tensor(out=bsc[:, 0:tw], in0=pg[0:16, bank, 0:tw],
                                       scalar=c01[:, 0:1], in1=maskw[:, 0:tw],
                                       op0=ALU.mult, op1=ALU.mult).then_inc(s_v, 1)
            else:
                v.tensor_scalar(out=bsc[:, 0:tw], in0=pg[0:16, bank, 0:tw],
                                scalar1=0.1, scalar2=None, op0=ALU.mult).then_inc(s_v, 1)
            v.wait_ge(s_v, base + 1)
            v.tensor_tensor_scan(out=bsc[:, 0:tw], data0=c09[:, 0:1].broadcast_to([S, tw]),
                                 data1=bsc[:, 0:tw], initial=carry[:, 0:1],
                                 op0=ALU.mult, op1=ALU.add).then_inc(s_v, 1)
            v.wait_ge(s_v, base + 2)
            v.tensor_copy(out=carry[:, 0:1], in_=bsc[:, tw - 1 : tw])
            v.tensor_copy(out=hsbf[:, 0:tw], in_=bsc[:, 0:tw]).then_inc(s_v, 1)

        ve(_f)
        C["v"] += 3
        bank_free[bank] = ("v", C["v"])

    def wc_y_round(tile, e, bank, v_hs, sc_z=None):
        _, tw = x_of(tile)

        def _mm(t):
            t.wait_ge(s_v, v_hs)
            wait_banks(t, [bank])
            t.matmul(pg[:, bank, 0:tw], wcT[:, e * 128 : (e + 1) * 128],
                     hsbf[:, 0:tw], start=True, stop=True).then_inc(s_pe, 1)

        pe(_mm)
        C["pe"] += 1
        base = C["v"]

        def _yev(v):
            v.wait_ge(s_pe, C["pe"])
            v.wait_ge(s_v, base)
            if sc_z is not None:
                v.wait_ge(s_sc, sc_z)
            v.scalar_tensor_tensor(out=scr[:, 0, 0:tw], in0=xpbf[:, e, 0:tw],
                                   scalar=dvec[:, e : e + 1], in1=pg[:, bank, 0:tw],
                                   op0=ALU.mult, op1=ALU.add).then_inc(s_v, 1)
            v.wait_ge(s_v, base + 1)
            v.tensor_tensor(out=zsbf[:, e, 0:tw], in0=scr[:, 0, 0:tw],
                            in1=zsbf[:, e, 0:tw], op=ALU.mult).then_inc(s_v, 1)

        ve(_yev)
        C["v"] += 2
        bank_free[bank] = ("v", C["v"])

    def wout_round(tile, m, bank, v_ready):
        xsrc, tw = x_of(tile)

        def _mm(t):
            t.wait_ge(s_v, v_ready)
            wait_banks(t, [bank])
            for kc in range(EC):
                mm = t.matmul(pg[:, bank, 0:tw], woT[:, kc, m * 128 : (m + 1) * 128],
                              zsbf[:, kc, 0:tw], start=(kc == 0), stop=(kc == EC - 1))
            mm.then_inc(s_pe, 1)

        pe(_mm)
        C["pe"] += 1

        def _evac(v):
            v.wait_ge(s_pe, C["pe"])
            v.scalar_tensor_tensor(out=xsrc[:, m, :], in0=pg[:, bank, 0:tw],
                                   scalar=b_out[:, m : m + 1], in1=xsrc[:, m, :],
                                   op0=ALU.add, op1=ALU.add).then_inc(s_v, 1)

        ve(_evac)
        C["v"] += 1
        bank_free[bank] = ("v", C["v"])

    def ffn2_round(tile, buf, m, bank, v_ready):
        ffn_round(tile, buf, m, bank, v_ready)

    loads_issued = {}

    xT_flat = d_xT.rearrange("d t -> (d t)")
    outT_flat = d_outT.rearrange("d t -> (d t)")

    def emit_load(tile, extra_v_wait=None):
        warm = tile == 0
        par = tile % 2
        tw = W if warm else TT
        base = 0 if warm else D * (W + (tile - 1) * TT)

        def _load(sync, par=par, tw=tw, base=base, tile=tile, ev=extra_v_wait):
            if ev is not None:
                sync.wait_ge(s_v, ev)
            if tile >= 3:
                sync.wait_ge(d_out, 16 * (tile - 2))
            sync.dma_start(out=xb[:, par, :, 0:tw],
                           in_=xT_flat[base : base + D * tw].rearrange(
                               "(c p t) -> p c t", p=128, t=tw)).then_inc(d_xt[tile], 16)

        sy(_load)
        loads_issued[tile] = True

    def emit_store(tile):
        def _store(sync, par=tile % 2, tile=tile, vv=C["v"]):
            sync.wait_ge(s_v, vv)
            sync.dma_start(out=outT_flat[(tile - 1) * D * TT : tile * D * TT].rearrange(
                               "(c p t) -> p c t", p=128, t=TT),
                           in_=xb[:, par, :, :]).then_inc(d_out, 16)

        sy(_store)
        C["dout"] += 16

    def wait_x(tile, extra_wg=False):
        def _f(v, tile=tile, wg=extra_wg):
            v.wait_ge(d_xt[tile], 16)
            if wg:
                v.wait_ge(d_wg, 64)

        ve(_f)


    def front_ln1(tile, buf, sbanks, cast_deps=(), cast_vdep=None):
        ln_cast(tile, buf, deps=cast_deps, vdep=cast_vdep)
        ln_summ(tile, buf, sbanks[0])
        ln_sumsq(tile, buf, sbanks[1])
        ln_rows(tile, sbanks)
        ln_sqrt(tile)
        ln_fin(tile)
        ln_bcast(tile, sbanks)
        ln_copyrb(tile, sbanks)
        ln_apply(tile, buf, sbanks)

    ve(_quant)
    C["v"] += 9
    v_quant = C["v"]
    emit_load(0, extra_v_wait=QV["C"])
    emit_load(1, extra_v_wait=QV["A"])
    wait_x(0)
    front_ln1(0, lnbf, (6, 7), cast_deps=((d_xt[0], 16),), cast_vdep=0)
    v_warm_ln1 = C["v"]
    for m in range(DC):
        ffn_round(0, lnbf, m, 4 + (m % 2), v_warm_ln1)
    front_ln1(0, lnbf, (6, 7), cast_deps=(("pe", C["pe"]),))
    v_warm_ln2 = C["v"]
    wait_x(1)
    front_ln1(1, lnA, (2, 3), cast_deps=((d_xt[1], 16),))
    v_f1_ln1 = C["v"]

    def _wg48(t):
        t.wait_ge(d_wg, 80)

    pe(_wg48)
    for m in range(EC):
        win_round(0, lnbf, m, 4 + (m % 2), v_warm_ln2, warm=True)
        if m == 13:
            sc_warm_wb_head = C["sc"]
    wb_mm(0, 6, warm=True, head_sc=sc_warm_wb_head)
    scan_stage(0, 6, warm=True)
    for m in range(DC):
        ffn_round(1, lnA, m, (m % 2), v_f1_ln1)
    front_ln1(1, lnbf, (2, 3), cast_deps=(("pe", C["pe"]),))
    v_f1_ln2 = C["v"]
    for m in range(EC):
        win_round(1, lnbf, m, (m % 2), v_f1_ln2, warm=False)
        if m == 13:
            sc_f1_wb_head = C["sc"]
    wb_mm(1, 2, warm=False, head_sc=sc_f1_wb_head)
    scan_stage(1, 2, warm=False)
    v_hs = C["v"]
    emit_load(2, extra_v_wait=C["v"])
    for e in range(EC):
        win_round(1, lnbf, EC + e, (e % 2), v_f1_ln2, warm=False)
        wc_y_round(1, e, 4 + (e % 2), v_hs, sc_z=C["sc"])
    scq_flush()

    def _wg64(t):
        t.wait_ge(d_wo, 16)

    pe(_wg64)

    for t in range(1, NT + 1):
        nxt = t + 1 if t < NT else None
        y_done = C["v"]
        if nxt is not None:
            ln_cast(nxt, lnbf, deps=(("pe", C["pe"]), (d_xt[nxt], 16)))
        wout_round(t, 0, 4, y_done)
        wout_round(t, 1, 5, y_done)
        if nxt is not None:
            ln_summ(nxt, lnbf, 2)
        wout_round(t, 2, 6, y_done)
        wout_round(t, 3, 7, y_done)
        v_w3 = C["v"]
        if nxt is not None:
            ln_sumsq(nxt, lnbf, 3)
        if nxt is not None:
            ln_rows(nxt, (2, 3))
            ln_sqrt(nxt)
            ln_fin(nxt)
            v_fin1 = C["v"]
        wout_round(t, 4, 4, y_done)
        wout_round(t, 5, 5, y_done)
        if nxt is not None:
            ln_bcast(nxt, (2, 3), v_cnt=v_fin1)
            ln_copyrb(nxt, (2, 3))
        wout_round(t, 6, 6, y_done)
        if nxt is not None:
            ln_apply(nxt, lnbf, (2, 3))
            v_ln1_n = C["v"]
        wout_round(t, 7, 7, y_done)
        v_w7 = C["v"]
        if nxt is None:
            par = t % 2
            cq0 = scq_add([("v", v_w3)],
                          lambda s, par=par: s.activation(out=lnA[:, 0:4, :], in_=xb[:, par, 0:4, :],
                                                          func=AF.Copy, bias=0.0, scale=1.0))
            cq1 = scq_add([("v", v_w7)],
                          lambda s, par=par: s.activation(out=lnA[:, 4:8, :], in_=xb[:, par, 4:8, :],
                                                          func=AF.Copy, bias=0.0, scale=1.0))

            def _sm0(t_, c=cq0):
                t_.wait_ge(s_sc, c)
                wait_banks(t_, [6])
                for kc in range(4):
                    t_.matmul(pg[0:1, 6, :], onesc[:, 0:1], lnA[:, kc, :],
                              start=(kc == 0), stop=False)

            pe(_sm0)

            def _sq0(v, c=cq0):
                v.wait_ge(s_sc, c)
                v.tensor_tensor(out=sqbf[:, :, :], in0=lnA[:, 0:4, :],
                                in1=lnA[:, 0:4, :], op=ALU.mult).then_inc(s_v, 1)

            ve(_sq0)
            C["v"] += 1

            def _sqm0(t_):
                t_.wait_ge(s_v, C["v"])
                wait_banks(t_, [7])
                for j in range(4):
                    mm = t_.matmul(pg[0:1, 7, :], onesc[:, 0:1], sqbf[:, j, :],
                                   start=(j == 0), stop=False)
                mm.then_inc(s_pe, 1)

            pe(_sqm0)
            C["pe"] += 1

            def _sm1(t_, c=cq1):
                t_.wait_ge(s_sc, c)
                for kc in range(4, 8):
                    mm = t_.matmul(pg[0:1, 6, :], onesc[:, 0:1], lnA[:, kc, :],
                                   start=False, stop=(kc == 7))
                mm.then_inc(s_pe, 1)

            pe(_sm1)
            C["pe"] += 1

            def _sq1(v, c=cq1, pcnt=C["pe"]):
                v.wait_ge(s_sc, c)
                v.wait_ge(s_pe, pcnt)
                v.tensor_tensor(out=sqbf[:, :, :], in0=lnA[:, 4:8, :],
                                in1=lnA[:, 4:8, :], op=ALU.mult).then_inc(s_v, 1)

            ve(_sq1)
            C["v"] += 1

            def _sqm1(t_):
                t_.wait_ge(s_v, C["v"])
                for j in range(4):
                    mm = t_.matmul(pg[0:1, 7, :], onesc[:, 0:1], sqbf[:, j, :],
                                   start=False, stop=(j == 3))
                mm.then_inc(s_pe, 1)

            pe(_sqm1)
            C["pe"] += 1
        else:
            ln_cast(t, lnA, deps=(("pe", C["pe"]),))
            ffn_round(nxt, lnbf, 0, 0, v_ln1_n)
            ffn_round(nxt, lnbf, 1, 1, v_ln1_n)
            ln_summ(t, lnA, 6)
            ffn_round(nxt, lnbf, 2, 0, v_ln1_n)
            ffn_round(nxt, lnbf, 3, 1, v_ln1_n)
            ln_sumsq(t, lnA, 7)
        ln_rows(t, (6, 7))
        ln_sqrt(t)
        ln_fin(t)
        v_fin3 = C["v"]
        if nxt is not None:
            ffn_round(nxt, lnbf, 4, 0, v_ln1_n)
            ffn_round(nxt, lnbf, 5, 1, v_ln1_n)
        ln_bcast(t, (6, 7), v_cnt=v_fin3)
        ln_copyrb(t, (6, 7))
        if nxt is not None:
            ffn_round(nxt, lnbf, 6, 0, v_ln1_n)
        ln_apply(t, lnA, (6, 7))
        v_ln3 = C["v"]
        if nxt is not None:
            ffn_round(nxt, lnbf, 7, 1, v_ln1_n)
        if nxt is not None:
            ln_cast(nxt, lnbf, deps=(("pe", C["pe"]),))
        ffn_round(t, lnA, 0, 4, v_ln3)
        ffn_round(t, lnA, 1, 5, v_ln3)
        if nxt is not None:
            ln_summ(nxt, lnbf, 2)
        ffn_round(t, lnA, 2, 4, v_ln3)
        ffn_round(t, lnA, 3, 5, v_ln3)
        if nxt is not None:
            ln_sumsq(nxt, lnbf, 3)
        if nxt is not None:
            ln_rows(nxt, (2, 3))
            ln_sqrt(nxt)
            ln_fin(nxt)
            v_fin2 = C["v"]
        ffn_round(t, lnA, 4, 4, v_ln3)
        ffn_round(t, lnA, 5, 5, v_ln3)
        if nxt is not None:
            ln_bcast(nxt, (2, 3), v_cnt=v_fin2)
            ln_copyrb(nxt, (2, 3))
        ffn_round(t, lnA, 6, 4, v_ln3)
        if nxt is not None:
            ln_apply(nxt, lnbf, (2, 3))
            v_ln2_n = C["v"]
        ffn_round(t, lnA, 7, 5, v_ln3)
        emit_store(t)
        if nxt is not None:
            for m in range(EC):
                win_round(nxt, lnbf, m, (m % 2), v_ln2_n, warm=False)
                if m == 13:
                    sc_wb_head = C["sc"]
            wb_mm(nxt, 2, warm=False)
            scan_stage(nxt, 2, warm=False)
            v_hs = C["v"]
            if nxt + 1 <= NT:
                emit_load(nxt + 1)
            for e in range(EC):
                win_round(nxt, lnbf, EC + e, (e % 2), v_ln2_n, warm=False)
                wc_y_round(nxt, e, 4 + (e % 2), v_hs, sc_z=C["sc"])
        scq_flush()

    def _final(g):
        g.wait_ge(d_out, C["dout"])

    gp(_final)

    es.close()
    return nc


def kernel(L0_attn_x, L0_attn_W_in, L0_attn_b_in, L0_attn_W_B, L0_attn_W_C,
           L0_attn_D, L0_attn_W_out, L0_attn_b_out, L0_ffn_weight):
    x = np.asarray(L0_attn_x)
    winT = np.ascontiguousarray(np.asarray(L0_attn_W_in).T)
    wfT = np.ascontiguousarray(np.asarray(L0_ffn_weight).T)
    wbT = np.ascontiguousarray(np.asarray(L0_attn_W_B).T)
    wcT = np.ascontiguousarray(np.asarray(L0_attn_W_C).T)
    woT = np.ascontiguousarray(np.asarray(L0_attn_W_out).T)
    b_in_pc = np.ascontiguousarray(np.asarray(L0_attn_b_in).reshape(D2 // 128, 128).T)
    b_out_pc = np.ascontiguousarray(np.asarray(L0_attn_b_out).reshape(DC, 128).T)
    d_pc = np.ascontiguousarray(np.asarray(L0_attn_D).reshape(EC, 128).T)

    B, T, _ = x.shape
    CPB = NCORES // B
    CT = T // CPB

    in_maps = []
    for k in range(NCORES):
        b, j = divmod(k, CPB)
        start = j * CT
        xs = np.zeros((D, TC), np.float32)
        xs[:, W:] = x[b, start : start + CT, :].T
        if j > 0:
            xs[:, :W] = x[b, start - W : start, :].T
            mw = np.full((S, W), np.float32(1.0), np.float32)
        else:
            mw = np.zeros((S, W), np.float32)
        blocks = [xs[:, 0:W].reshape(DC, 128, W)]
        for tt in range(NT):
            blocks.append(xs[:, W + tt * TT : W + (tt + 1) * TT].reshape(DC, 128, TT))
        xs = np.concatenate([blk.ravel() for blk in blocks]).reshape(D, TC)
        in_maps.append({
            "xT": np.ascontiguousarray(xs),
            "wfT": wfT, "winT": winT, "wbT": wbT, "wcT": wcT, "woT": woT,
            "b_in_pc": b_in_pc, "b_out_pc": b_out_pc, "d_pc": d_pc,
            "mask_w": mw,
        })

    nc = build_program()
    res = run_bass_kernel_spmd(nc, in_maps, core_ids=list(range(NCORES)))

    out = np.empty((B, T, D), np.float32)
    for k in range(NCORES):
        b, j = divmod(k, CPB)
        blocks = np.asarray(res.results[k]["outT"]).reshape(NT, DC, 128, TT)
        for tt in range(NT):
            out[b, j * CT + tt * TT : j * CT + (tt + 1) * TT, :] = blocks[tt].reshape(D, TT).T
    return out


# version 48
# speedup vs baseline: 1.0432x; 1.0432x over previous
import contextlib

import numpy as np

import concourse.bass as bass
import concourse.mybir as mybir
from concourse.bass_utils import run_bass_kernel_spmd
from concourse import dve_ops

F32 = mybir.dt.float32
BF16 = mybir.dt.bfloat16
AF = mybir.ActivationFunctionType
ALU = mybir.AluOpType
AX = mybir.AxisListType

D = 1024
DC = 8
D2 = 4096
DI = 2048
EC = 16
S = 16
TT = 512
NT = 4
W = 192
TC = W + NT * TT
NCORES = 8
EPS = 1e-5


def build_program():
    nc = bass.Bass("TRN2", target_bir_lowering=False)

    d_xT = nc.dram_tensor("xT", [D, TC], F32, kind="ExternalInput").ap()
    d_wfT = nc.dram_tensor("wfT", [D, D], F32, kind="ExternalInput").ap()
    d_winT = nc.dram_tensor("winT", [D, D2], F32, kind="ExternalInput").ap()
    d_wbT = nc.dram_tensor("wbT", [DI, S], F32, kind="ExternalInput").ap()
    d_wcT = nc.dram_tensor("wcT", [S, DI], F32, kind="ExternalInput").ap()
    d_woT = nc.dram_tensor("woT", [DI, D], F32, kind="ExternalInput").ap()
    d_bin = nc.dram_tensor("b_in_pc", [128, D2 // 128], F32, kind="ExternalInput").ap()
    d_bout = nc.dram_tensor("b_out_pc", [128, DC], F32, kind="ExternalInput").ap()
    d_dvec = nc.dram_tensor("d_pc", [128, EC], F32, kind="ExternalInput").ap()
    d_mask = nc.dram_tensor("mask_w", [S, W], F32, kind="ExternalInput").ap()
    d_outT = nc.dram_tensor("outT", [D, NT * TT], F32, kind="ExternalOutput").ap()

    es = contextlib.ExitStack()
    sb = lambda name, shape, dt: es.enter_context(nc.sbuf_tensor(name, shape, dt))
    psum = lambda name, shape, dt: es.enter_context(nc.psum_tensor(name, shape, dt))
    sem = lambda name: es.enter_context(nc.semaphore(name))

    xb = sb("xb", [128, 2, DC, TT], F32)
    winT = sb("winT_bf", [128, DC, D2], BF16)
    woT = sb("woT_bf", [128, EC, D], BF16)
    wfT = sb("wfT_bf", [128, DC, D], BF16)
    wbT = sb("wbT_bf", [128, EC, S], BF16)
    wcT = sb("wcT_bf", [S, DI], BF16)
    xz = sb("xz_bf", [128, 2, EC, TT], BF16)
    lnbf = sb("lnbf", [128, DC, TT], BF16)
    scr = sb("scr", [128, 1, TT], F32)
    rows = sb("rows", [1, 2, TT], F32)
    rows_bf = sb("rows_bf", [1, 2, TT], BF16)
    rbm_bf = sb("rbm_bf", [128, 2, TT], BF16)
    constz = sb("constz", [128, 1], F32)
    b_in = sb("b_in", [128, D2 // 128], F32)
    b_out = sb("b_out", [128, DC], F32)
    dvec = sb("dvec", [128, EC], F32)
    maskw = sb("maskw", [S, W], BF16)
    bsc = sb("bsc", [S, TT], F32)
    hsbf = sb("hs_bf", [S, TT], BF16)
    carry = sb("carry", [S, 1], F32)
    c09 = sb("c09", [S, 1], F32)
    c01 = sb("c01", [S, 1], F32)
    onesr_bf = sb("onesr_bf", [1, 128], BF16)
    onesr = sb("onesr", [1, 128], F32)
    onesc = sb("onesc", [128, 1], BF16)
    onescf = sb("onescf", [128, 1], F32)
    epsap = sb("epsap", [1, 1], F32)
    absc = sb("absc", [128, 2], F32)
    sc05 = sb("sc05", [128, 1], F32)
    rsc = sb("rsc", [128, 1], F32)
    srow = sb("srow", [1, 8], F32)

    pg = psum("pg", [128, 8, TT], F32)

    sqbf = sb("sqbf", [128, 4, TT], BF16)
    xpbf = xz[:, 0, :, :]
    zsbf = xz[:, 1, :, :]
    lnA = xz[:, 0, 0:DC, :]
    wfA = xb[:, 1, :, :].rearrange("p c t -> p (c t)")
    wfB = xb[:, 0, :, 256:512]
    wfC = xb[:, 0, :, 0:256]

    s_pe = sem("s_pe")
    s_v = sem("s_v")
    s_sc = sem("s_sc")
    d_s = sem("d_s")
    d_wg = sem("d_wg")
    d_mk = sem("d_mk")
    d_wo = sem("d_wo")
    d_wi = [sem(f"d_wi{j}") for j in range(8)]
    d_out = sem("d_out")
    d_xt = [sem(f"d_x{t}") for t in range(NT + 1)]

    C = {"pe": 0, "v": 0, "sc": 0, "dout": 0}
    SEMS = {"pe": None, "v": None, "sc": None}

    blk = es.enter_context(nc.Block())
    pe, ve, sc, gp, sy = blk.tensor, blk.vector, blk.scalar, blk.gpsimd, blk.sync

    def _sem(kind):
        return {"pe": s_pe, "v": s_v, "sc": s_sc}[kind]

    scq_items = []

    def scq_add(waits, fn):
        C["sc"] += 1
        scq_items.append((list(waits), fn))
        return C["sc"]

    def scq_flush():
        if not scq_items:
            return
        items = list(scq_items)
        del scq_items[:]

        def _sec(s):
            for waits, fn in items:
                for kind, cnt in waits:
                    s.wait_ge(_sem(kind) if isinstance(kind, str) else kind, cnt)
                fn(s).then_inc(s_sc, 1)

        sc(_sec)

    bank_free = {b: ("v", 0) for b in range(8)}

    def wait_banks(eng, banks):
        need = {}
        for b in banks:
            kind, cnt = bank_free[b]
            if cnt > 0:
                need[kind] = max(need.get(kind, 0), cnt)
        for kind, cnt in need.items():
            eng.wait_ge(_sem(kind), cnt)

    def _loads(sync):
        for c in range(4):
            sync.dma_start(out=xb[:, 1, 2 * c : 2 * c + 2, :],
                           in_=d_wfT[128 * c : 128 * c + 128, :].rearrange(
                               "p (a t) -> p a t", t=512)).then_inc(d_s, 16)

    sy(_loads)

    def _loads2(s):
        for c in range(2):
            s.dma_start(out=xb[:, 0, 4 * c : 4 * c + 4, 0:256],
                        in_=d_wfT[768 + 128 * c : 896 + 128 * c, :].rearrange(
                            "p (a t) -> p a t", t=256)).then_inc(d_s, 16)
        for c in range(2):
            s.dma_start(out=xb[:, 0, 4 * c : 4 * c + 4, 256:512],
                        in_=d_wfT[512 + 128 * c : 640 + 128 * c, :].rearrange(
                            "p (a t) -> p a t", t=256)).then_inc(d_s, 16)

    sc(_loads2)

    def _casts(g):
        g.dma_start(out=maskw[:], in_=d_mask).then_inc(d_mk, 16)
        g.dma_start(out=b_in[:], in_=d_bin).then_inc(d_wg, 16)
        g.dma_start(out=b_out[:], in_=d_bout).then_inc(d_wg, 16)
        g.dma_start(out=dvec[:], in_=d_dvec).then_inc(d_wg, 16)
        g.dma_start(out=wbT[:], in_=d_wbT.rearrange("(c p) s -> p c s", p=128)).then_inc(d_wg, 16)
        g.dma_start(out=wcT[:], in_=d_wcT).then_inc(d_wg, 16)
        g.wait_ge(d_s, 8 * 16)
        for j in range(8):
            if j == 2:
                g.wait_ge(d_xt[1], 16)
            g.dma_start(out=winT[:, :, 512 * j : 512 * (j + 1)],
                        in_=d_winT.rearrange("(c p) o -> p c o", p=128)[:, :, 512 * j : 512 * (j + 1)]
                        ).then_inc(d_wi[j], 16)
        g.dma_start(out=woT[:], in_=d_woT.rearrange("(c p) o -> p c o", p=128)).then_inc(d_wo, 16)

    gp(_casts)

    def _consts(v):
        v.memset(onesr[:], 1.0)
        v.memset(onesr_bf[:], 1.0)
        v.memset(onesc[:], 1.0)
        v.memset(onescf[:], 1.0)
        v.memset(epsap[:], EPS)
        v.memset(constz[:], 0.0)
        v.memset(carry[:], 0.0)
        v.memset(c09[:], 9.0)
        v.memset(absc[:], 0.0)
        v.memset(c01[:], 10.0).then_inc(s_v, 1)

    ve(_consts)
    C["v"] += 1

    def _cdiv(v):
        v.wait_ge(s_v, C["v"])
        v.reciprocal(out=c01[:], in_=c01[:]).then_inc(s_v, 1)
        v.wait_ge(s_v, C["v"] + 1)
        v.tensor_scalar(out=c09[:], in0=c01[:], scalar1=9.0, scalar2=None, op0=ALU.mult).then_inc(s_v, 1)

    ve(_cdiv)
    C["v"] += 2

    zq = xz[:, 1, :, :].rearrange("p c t -> p (c t)")

    wfBC = xb[:, 0, :, :].rearrange("p c t -> p (c t)")

    def _abssum(s, vw=C["v"]):
        s.wait_ge(s_v, vw)
        s.wait_ge(d_s, 8 * 16)
        s.activation(out=zq[:, 0:4096], in_=wfA, func=AF.Abs, bias=0.0,
                     scale=1.0, accum_out=absc[:, 0:1])
        s.activation(out=zq[:, 4096:8192], in_=wfBC, func=AF.Abs, bias=0.0,
                     scale=1.0, accum_out=absc[:, 1:2]).then_inc(s_sc, 1)

    sc(_abssum)
    C["sc"] += 1

    def _absmm(t):
        t.wait_ge(s_sc, C["sc"])
        t.wait_ge(s_v, C["v"])
        t.matmul(pg[0:1, 5, 0:2], onescf[:, 0:1], absc[:, 0:2], start=True, stop=True).then_inc(s_pe, 1)

    pe(_absmm)
    C["pe"] += 1

    def _scale(v):
        v.wait_ge(s_pe, C["pe"])
        v.tensor_copy(out=srow[0:1, 0:2], in_=pg[0:1, 5, 0:2]).then_inc(s_v, 1)
        v.wait_ge(s_v, C["v"] + 1)
        v.tensor_tensor(out=srow[0:1, 0:1], in0=srow[0:1, 0:1], in1=srow[0:1, 1:2], op=ALU.add).then_inc(s_v, 1)
        v.wait_ge(s_v, C["v"] + 2)
        v.tensor_scalar(out=srow[0:1, 2:3], in0=srow[0:1, 0:1], scalar1=1.0 / (D * D), scalar2=None, op0=ALU.mult).then_inc(s_v, 1)
        v.wait_ge(s_v, C["v"] + 3)
        v.tensor_scalar(out=srow[0:1, 3:4], in0=srow[0:1, 2:3], scalar1=EPS, scalar2=0.5, op0=ALU.max, op1=ALU.mult).then_inc(s_v, 1)
        v.wait_ge(s_v, C["v"] + 4)
        v.tensor_scalar(out=srow[0:1, 4:5], in0=srow[0:1, 3:4], scalar1=-1.0, scalar2=None, op0=ALU.mult).then_inc(s_v, 1)

    ve(_scale)
    C["v"] += 5

    def _scalebc(t):
        t.wait_ge(s_v, C["v"])
        t.matmul(pg[:, 4, 0:1], onesr[0:1, :], srow[0:1, 3:4], start=True, stop=True)
        mm = t.matmul(pg[:, 4, 1:2], onesr[0:1, :], srow[0:1, 4:5], start=True, stop=True)
        mm.then_inc(s_pe, 1)

    pe(_scalebc)
    C["pe"] += 1

    def _scalecp(v):
        v.wait_ge(s_pe, C["pe"])
        v.tensor_copy(out=sc05[:], in_=pg[:, 4, 0:1])
        v.tensor_copy(out=rsc[:], in_=pg[:, 4, 1:2]).then_inc(s_v, 1)

    ve(_scalecp)
    C["v"] += 1
    bank_free[4] = ("v", C["v"])
    bank_free[5] = ("v", C["v"])

    wfT_flat = wfT[:, :, :].rearrange("p c o -> p (c o)")
    QV = {}

    def _quant(v):
        n = 0
        for name, qsrc, lo, hi in (("C", wfC, 6144, 8192), ("A", wfA, 0, 4096),
                                   ("B", wfB, 4096, 6144)):
            dst = wfT_flat[:, lo:hi]
            tmp = zq[:, 0 : hi - lo]
            if name in ("B", "C"):
                dst = dst.rearrange("p (a t) -> p a t", t=256)
                tmp = tmp.rearrange("p (a t) -> p a t", t=256)
            v.wait_ge(s_v, C["v"] + n)
            v.tensor_scalar(out=tmp, in0=qsrc, scalar1=rsc[:, 0:1], scalar2=None, op0=ALU.is_le).then_inc(s_v, 1)
            v.tensor_scalar(out=dst, in0=qsrc, scalar1=sc05[:, 0:1], scalar2=None, op0=ALU.is_ge).then_inc(s_v, 1)
            QV[name] = C["v"] + n + 2
            v.wait_ge(s_v, C["v"] + n + 2)
            v.tensor_tensor(out=dst, in0=dst, in1=tmp, op=ALU.subtract).then_inc(s_v, 1)
            n += 3

    def x_of(tile):
        tw = W if tile == 0 else TT
        return xb[:, tile % 2, :, 0:tw], tw

    rowA = rows[0:1, 0, :]
    rowB = rows[0:1, 1, :]
    rowS = scr[0:1, 0, :]

    def ln_cast(tile, buf, deps=(), vdep=None):
        xsrc, tw = x_of(tile)
        vw = [] if vdep == 0 else [("v", C["v"] if vdep is None else vdep)]
        scq_add(vw + list(deps),
                lambda s, buf=buf, xsrc=xsrc, tw=tw: s.activation(
                    out=buf[:, :, 0:tw], in_=xsrc, func=AF.Copy, bias=0.0, scale=1.0))

    def ln_summ(tile, buf, bank):
        _, tw = x_of(tile)

        def _f(t, sc_cnt=C["sc"]):
            t.wait_ge(s_sc, sc_cnt)
            wait_banks(t, [bank])
            for kc in range(DC):
                mm = t.matmul(pg[0:1, bank, 0:tw], onesc[:, 0:1], buf[:, kc, 0:tw],
                              start=(kc == 0), stop=(kc == DC - 1))
            mm.then_inc(s_pe, 1)

        pe(_f)
        C["pe"] += 1

    def ln_sumsq(tile, buf, bank):
        _, tw = x_of(tile)
        for q in range(2):
            def _sq(v, q=q, sc_cnt=C["sc"], pe_cnt=C["pe"]):
                if q == 0:
                    v.wait_ge(s_sc, sc_cnt)
                else:
                    v.wait_ge(s_pe, pe_cnt)
                v.tensor_tensor(out=sqbf[:, :, 0:tw], in0=buf[:, 4 * q : 4 * q + 4, 0:tw],
                                in1=buf[:, 4 * q : 4 * q + 4, 0:tw], op=ALU.mult).then_inc(s_v, 1)

            ve(_sq)
            C["v"] += 1

            def _mm(t, q=q):
                t.wait_ge(s_v, C["v"])
                if q == 0:
                    wait_banks(t, [bank])
                for j in range(4):
                    mm = t.matmul(pg[0:1, bank, 0:tw], onesc[:, 0:1], sqbf[:, j, 0:tw],
                                  start=(q == 0 and j == 0), stop=(q == 1 and j == 3))
                mm.then_inc(s_pe, 1)

            pe(_mm)
            C["pe"] += 1

    def ln_rows(tile, banks):
        _, tw = x_of(tile)
        b0, b1 = banks

        def _f(v):
            v.wait_ge(s_pe, C["pe"])
            v.tensor_scalar(out=rows[0:1, :, 0:tw], in0=pg[0:1, b0 : b0 + 2, 0:tw],
                            scalar1=1.0 / D, scalar2=None, op0=ALU.mult).then_inc(s_v, 1)
            v.wait_ge(s_v, C["v"] + 1)
            v.tensor_tensor(out=rowS[:, 0:tw], in0=rowA[:, 0:tw], in1=rowA[:, 0:tw], op=ALU.mult).then_inc(s_v, 1)
            v.wait_ge(s_v, C["v"] + 2)
            v.tensor_tensor(out=rowB[:, 0:tw], in0=rowB[:, 0:tw], in1=rowS[:, 0:tw], op=ALU.subtract).then_inc(s_v, 1)

        ve(_f)
        C["v"] += 3

    def ln_sqrt(tile):
        _, tw = x_of(tile)
        scq_add([("v", C["v"])],
                lambda s, tw=tw: s.activation(out=rowB[:, 0:tw], in_=rowB[:, 0:tw],
                                              func=AF.Ln, bias=epsap[0:1, 0:1], scale=1.0))
        scq_add([("sc", C["sc"])],
                lambda s, tw=tw: s.activation(out=rowB[:, 0:tw], in_=rowB[:, 0:tw],
                                              func=AF.Exp, bias=0.0, scale=-0.5))

    def ln_fin(tile):
        _, tw = x_of(tile)

        def _f(v):
            v.wait_ge(s_sc, C["sc"])
            v.tensor_tensor(out=rowA[:, 0:tw], in0=rowA[:, 0:tw], in1=rowB[:, 0:tw], op=ALU.mult).then_inc(s_v, 1)
            v.wait_ge(s_v, C["v"] + 1)
            v.tensor_copy(out=rows_bf[0:1, :, 0:tw], in_=rows[0:1, :, 0:tw]).then_inc(s_v, 1)

        ve(_f)
        C["v"] += 2

    def ln_bcast(tile, banks, v_cnt=None):
        _, tw = x_of(tile)
        b0, b1 = banks
        vw = C["v"] if v_cnt is None else v_cnt

        def _f(t):
            t.wait_ge(s_v, vw)
            wait_banks(t, [b0, b1])
            t.matmul(pg[:, b0, 0:tw], onesr_bf[0:1, :], rows_bf[0:1, 1, 0:tw], start=True, stop=True)
            mm = t.matmul(pg[:, b1, 0:tw], onesr_bf[0:1, :], rows_bf[0:1, 0, 0:tw], start=True, stop=True)
            mm.then_inc(s_pe, 1)

        pe(_f)
        C["pe"] += 1

    def ln_copyrb(tile, banks):
        _, tw = x_of(tile)
        b0, b1 = banks
        scq_add([("pe", C["pe"])],
                lambda s, tw=tw, b0=b0: s.activation(out=rbm_bf[:, 0, 0:tw], in_=pg[:, b0, 0:tw],
                                                     func=AF.Copy, bias=0.0, scale=1.0))
        cnt = scq_add([],
                      lambda s, tw=tw, b1=b1: s.activation(out=rbm_bf[:, 1, 0:tw], in_=pg[:, b1, 0:tw],
                                                           func=AF.Copy, bias=0.0, scale=1.0))
        bank_free[b0] = ("sc", cnt)
        bank_free[b1] = ("sc", cnt)

    def ln_apply(tile, buf, banks):
        _, tw = x_of(tile)
        base = C["v"]

        def _f(v):
            v.wait_ge(s_sc, C["sc"])
            v.tensor_tensor(out=buf[:, :, 0:tw], in0=buf[:, :, 0:tw],
                            in1=rbm_bf[:, 0:1, 0:tw].broadcast_to([128, DC, tw]),
                            op=ALU.mult).then_inc(s_v, 1)
            v.wait_ge(s_v, base + 1)
            v.tensor_tensor(out=buf[:, :, 0:tw], in0=buf[:, :, 0:tw],
                            in1=rbm_bf[:, 1:2, 0:tw].broadcast_to([128, DC, tw]),
                            op=ALU.subtract).then_inc(s_v, 1)

        ve(_f)
        C["v"] += 2

    def ffn_round(tile, buf, m, bank, v_ready):
        xsrc, tw = x_of(tile)

        def _mm(t):
            t.wait_ge(s_v, v_ready)
            wait_banks(t, [bank])
            for kc in range(DC):
                mm = t.matmul(pg[:, bank, 0:tw], wfT[:, kc, m * 128 : (m + 1) * 128],
                              buf[:, kc, 0:tw], start=(kc == 0), stop=(kc == DC - 1))
            mm.then_inc(s_pe, 1)

        pe(_mm)
        C["pe"] += 1

        def _evac(v):
            v.wait_ge(s_pe, C["pe"])
            v.scalar_tensor_tensor(out=xsrc[:, m, :], in0=pg[:, bank, 0:tw],
                                   scalar=sc05[:, 0:1], in1=xsrc[:, m, :],
                                   op0=ALU.mult, op1=ALU.add).then_inc(s_v, 1)

        ve(_evac)
        C["v"] += 1
        bank_free[bank] = ("v", C["v"])

    def win_round(tile, buf, m, bank, v_ready, warm):
        _, tw = x_of(tile)

        def _mm(t):
            t.wait_ge(d_wi[m // 4], 16)
            t.wait_ge(s_v, v_ready)
            wait_banks(t, [bank])
            for kc in range(DC):
                mm = t.matmul(pg[:, bank, 0:tw], winT[:, kc, m * 128 : (m + 1) * 128],
                              buf[:, kc, 0:tw], start=(kc == 0), stop=(kc == DC - 1))
            mm.then_inc(s_pe, 1)

        pe(_mm)
        C["pe"] += 1

        if warm:
            out = zsbf[:, m, 0:tw] if m < DC else xpbf[:, m, 0:tw]
        else:
            out = xpbf[:, m, 0:tw] if m < EC else zsbf[:, m - EC, 0:tw]
        cnt = scq_add([("pe", C["pe"])],
                      lambda s, out=out, bank=bank, m=m, tw=tw: s.activation(
                          out=out, in_=pg[:, bank, 0:tw], func=AF.Silu,
                          bias=b_in[:, m : m + 1], scale=1.0))
        bank_free[bank] = ("sc", cnt)

    def wb_mm(tile, bank, warm, head_sc=None):
        _, tw = x_of(tile)

        def _rhs(kc):
            if warm:
                return zsbf[:, kc, 0:tw] if kc < DC else xpbf[:, kc, 0:tw]
            return xpbf[:, kc, 0:tw]

        def _head(t, hs=head_sc if head_sc is not None else C["sc"]):
            t.wait_ge(s_sc, hs)
            wait_banks(t, [bank])
            for kc in range(EC - 2):
                t.matmul(pg[0:16, bank, 0:tw], wbT[:, kc, :], _rhs(kc),
                         start=(kc == 0), stop=False)

        pe(_head)

        def _tail(t):
            t.wait_ge(s_sc, C["sc"])
            for kc in range(EC - 2, EC):
                mm = t.matmul(pg[0:16, bank, 0:tw], wbT[:, kc, :], _rhs(kc),
                              start=False, stop=(kc == EC - 1))
            mm.then_inc(s_pe, 1)

        pe(_tail)
        C["pe"] += 1

    def scan_stage(tile, bank, warm):
        _, tw = x_of(tile)
        base = C["v"]

        def _f(v):
            v.wait_ge(s_pe, C["pe"])
            if warm:
                v.wait_ge(d_mk, 16)
                v.scalar_tensor_tensor(out=bsc[:, 0:tw], in0=pg[0:16, bank, 0:tw],
                                       scalar=c01[:, 0:1], in1=maskw[:, 0:tw],
                                       op0=ALU.mult, op1=ALU.mult).then_inc(s_v, 1)
            else:
                v.tensor_scalar(out=bsc[:, 0:tw], in0=pg[0:16, bank, 0:tw],
                                scalar1=0.1, scalar2=None, op0=ALU.mult).then_inc(s_v, 1)
            v.wait_ge(s_v, base + 1)
            v.tensor_tensor_scan(out=bsc[:, 0:tw], data0=c09[:, 0:1].broadcast_to([S, tw]),
                                 data1=bsc[:, 0:tw], initial=carry[:, 0:1],
                                 op0=ALU.mult, op1=ALU.add).then_inc(s_v, 1)
            v.wait_ge(s_v, base + 2)
            v.tensor_copy(out=carry[:, 0:1], in_=bsc[:, tw - 1 : tw])
            v.tensor_copy(out=hsbf[:, 0:tw], in_=bsc[:, 0:tw]).then_inc(s_v, 1)

        ve(_f)
        C["v"] += 3
        bank_free[bank] = ("v", C["v"])

    def wc_y_round(tile, e, bank, v_hs, sc_z=None):
        _, tw = x_of(tile)

        def _mm(t):
            t.wait_ge(s_v, v_hs)
            wait_banks(t, [bank])
            t.matmul(pg[:, bank, 0:tw], wcT[:, e * 128 : (e + 1) * 128],
                     hsbf[:, 0:tw], start=True, stop=True).then_inc(s_pe, 1)

        pe(_mm)
        C["pe"] += 1
        base = C["v"]

        def _yev(v):
            v.wait_ge(s_pe, C["pe"])
            v.wait_ge(s_v, base)
            if sc_z is not None:
                v.wait_ge(s_sc, sc_z)
            v.scalar_tensor_tensor(out=scr[:, 0, 0:tw], in0=xpbf[:, e, 0:tw],
                                   scalar=dvec[:, e : e + 1], in1=pg[:, bank, 0:tw],
                                   op0=ALU.mult, op1=ALU.add).then_inc(s_v, 1)
            v.wait_ge(s_v, base + 1)
            v.tensor_tensor(out=zsbf[:, e, 0:tw], in0=scr[:, 0, 0:tw],
                            in1=zsbf[:, e, 0:tw], op=ALU.mult).then_inc(s_v, 1)

        ve(_yev)
        C["v"] += 2
        bank_free[bank] = ("v", C["v"])

    def wout_round(tile, m, bank, v_ready):
        xsrc, tw = x_of(tile)

        def _mm(t):
            t.wait_ge(s_v, v_ready)
            wait_banks(t, [bank])
            for kc in range(EC):
                mm = t.matmul(pg[:, bank, 0:tw], woT[:, kc, m * 128 : (m + 1) * 128],
                              zsbf[:, kc, 0:tw], start=(kc == 0), stop=(kc == EC - 1))
            mm.then_inc(s_pe, 1)

        pe(_mm)
        C["pe"] += 1

        def _evac(v):
            v.wait_ge(s_pe, C["pe"])
            v.scalar_tensor_tensor(out=xsrc[:, m, :], in0=pg[:, bank, 0:tw],
                                   scalar=b_out[:, m : m + 1], in1=xsrc[:, m, :],
                                   op0=ALU.add, op1=ALU.add).then_inc(s_v, 1)

        ve(_evac)
        C["v"] += 1
        bank_free[bank] = ("v", C["v"])

    def ffn2_round(tile, buf, m, bank, v_ready):
        ffn_round(tile, buf, m, bank, v_ready)

    loads_issued = {}

    xT_flat = d_xT.rearrange("d t -> (d t)")
    outT_flat = d_outT.rearrange("d t -> (d t)")

    def emit_load(tile, extra_v_wait=None):
        warm = tile == 0
        par = tile % 2
        tw = W if warm else TT
        base = 0 if warm else D * (W + (tile - 1) * TT)

        def _load(sync, par=par, tw=tw, base=base, tile=tile, ev=extra_v_wait):
            if ev is not None:
                sync.wait_ge(s_v, ev)
            if tile >= 3:
                sync.wait_ge(d_out, 16 * (tile - 2))
            sync.dma_start(out=xb[:, par, :, 0:tw],
                           in_=xT_flat[base : base + D * tw].rearrange(
                               "(c p t) -> p c t", p=128, t=tw)).then_inc(d_xt[tile], 16)

        sy(_load)
        loads_issued[tile] = True

    def emit_store(tile):
        def _store(sync, par=tile % 2, tile=tile, vv=C["v"]):
            sync.wait_ge(s_v, vv)
            sync.dma_start(out=outT_flat[(tile - 1) * D * TT : tile * D * TT].rearrange(
                               "(c p t) -> p c t", p=128, t=TT),
                           in_=xb[:, par, :, :]).then_inc(d_out, 16)

        sy(_store)
        C["dout"] += 16

    def wait_x(tile, extra_wg=False):
        def _f(v, tile=tile, wg=extra_wg):
            v.wait_ge(d_xt[tile], 16)
            if wg:
                v.wait_ge(d_wg, 64)

        ve(_f)


    def front_ln1(tile, buf, sbanks, cast_deps=(), cast_vdep=None):
        ln_cast(tile, buf, deps=cast_deps, vdep=cast_vdep)
        ln_summ(tile, buf, sbanks[0])
        ln_sumsq(tile, buf, sbanks[1])
        ln_rows(tile, sbanks)
        ln_sqrt(tile)
        ln_fin(tile)
        ln_bcast(tile, sbanks)
        ln_copyrb(tile, sbanks)
        ln_apply(tile, buf, sbanks)

    ve(_quant)
    C["v"] += 9
    v_quant = C["v"]
    emit_load(0, extra_v_wait=QV["C"])
    emit_load(1, extra_v_wait=QV["A"])
    wait_x(0)
    front_ln1(0, lnbf, (6, 7), cast_deps=((d_xt[0], 16),), cast_vdep=0)
    v_warm_ln1 = C["v"]
    for m in range(DC):
        ffn_round(0, lnbf, m, 4 + (m % 2), v_warm_ln1)
    front_ln1(0, lnbf, (6, 7), cast_deps=(("pe", C["pe"]),))
    v_warm_ln2 = C["v"]
    wait_x(1)
    front_ln1(1, lnA, (2, 3), cast_deps=((d_xt[1], 16),))
    v_f1_ln1 = C["v"]

    def _wg48(t):
        t.wait_ge(d_wg, 80)

    pe(_wg48)
    for m in range(EC):
        win_round(0, lnbf, m, 4 + (m % 2), v_warm_ln2, warm=True)
        if m == 13:
            sc_warm_wb_head = C["sc"]
    wb_mm(0, 6, warm=True, head_sc=sc_warm_wb_head)
    scan_stage(0, 6, warm=True)
    for m in range(DC):
        ffn_round(1, lnA, m, (m % 2), v_f1_ln1)
    front_ln1(1, lnbf, (2, 3), cast_deps=(("pe", C["pe"]),))
    v_f1_ln2 = C["v"]
    for m in range(EC):
        win_round(1, lnbf, m, (m % 2), v_f1_ln2, warm=False)
        if m == 13:
            sc_f1_wb_head = C["sc"]
    wb_mm(1, 2, warm=False, head_sc=sc_f1_wb_head)
    scan_stage(1, 2, warm=False)
    v_hs = C["v"]
    emit_load(2, extra_v_wait=C["v"])
    for e in range(EC):
        win_round(1, lnbf, EC + e, (e % 2), v_f1_ln2, warm=False)
        wc_y_round(1, e, 4 + (e % 2), v_hs, sc_z=C["sc"])
    scq_flush()

    def _wg64(t):
        t.wait_ge(d_wo, 16)

    pe(_wg64)

    for t in range(1, NT + 1):
        nxt = t + 1 if t < NT else None
        y_done = C["v"]
        if nxt is not None:
            ln_cast(nxt, lnbf, deps=(("pe", C["pe"]), (d_xt[nxt], 16)))
        wout_round(t, 0, 4, y_done)
        wout_round(t, 1, 5, y_done)
        if nxt is not None:
            ln_summ(nxt, lnbf, 2)
        wout_round(t, 2, 6, y_done)
        wout_round(t, 3, 7, y_done)
        v_w3 = C["v"]
        if nxt is not None:
            ln_sumsq(nxt, lnbf, 3)
        if nxt is not None:
            ln_rows(nxt, (2, 3))
            ln_sqrt(nxt)
            ln_fin(nxt)
            v_fin1 = C["v"]
        wout_round(t, 4, 4, y_done)
        wout_round(t, 5, 5, y_done)
        if nxt is not None:
            ln_bcast(nxt, (2, 3), v_cnt=v_fin1)
            ln_copyrb(nxt, (2, 3))
        wout_round(t, 6, 6, y_done)
        if nxt is not None:
            ln_apply(nxt, lnbf, (2, 3))
            v_ln1_n = C["v"]
        wout_round(t, 7, 7, y_done)
        v_w7 = C["v"]
        if nxt is None:
            par = t % 2
            cq0 = scq_add([("v", v_w3)],
                          lambda s, par=par: s.activation(out=lnA[:, 0:4, :], in_=xb[:, par, 0:4, :],
                                                          func=AF.Copy, bias=0.0, scale=1.0))
            cq1 = scq_add([("v", v_w7)],
                          lambda s, par=par: s.activation(out=lnA[:, 4:8, :], in_=xb[:, par, 4:8, :],
                                                          func=AF.Copy, bias=0.0, scale=1.0))

            def _sm0(t_, c=cq0):
                t_.wait_ge(s_sc, c)
                wait_banks(t_, [6])
                for kc in range(4):
                    t_.matmul(pg[0:1, 6, :], onesc[:, 0:1], lnA[:, kc, :],
                              start=(kc == 0), stop=False)

            pe(_sm0)

            def _sq0(v, c=cq0):
                v.wait_ge(s_sc, c)
                v.tensor_tensor(out=sqbf[:, :, :], in0=lnA[:, 0:4, :],
                                in1=lnA[:, 0:4, :], op=ALU.mult).then_inc(s_v, 1)

            ve(_sq0)
            C["v"] += 1

            def _sqm0(t_):
                t_.wait_ge(s_v, C["v"])
                wait_banks(t_, [7])
                for j in range(4):
                    mm = t_.matmul(pg[0:1, 7, :], onesc[:, 0:1], sqbf[:, j, :],
                                   start=(j == 0), stop=False)
                mm.then_inc(s_pe, 1)

            pe(_sqm0)
            C["pe"] += 1

            def _sm1(t_, c=cq1):
                t_.wait_ge(s_sc, c)
                for kc in range(4, 8):
                    mm = t_.matmul(pg[0:1, 6, :], onesc[:, 0:1], lnA[:, kc, :],
                                   start=False, stop=(kc == 7))
                mm.then_inc(s_pe, 1)

            pe(_sm1)
            C["pe"] += 1

            def _sq1(v, c=cq1, pcnt=C["pe"]):
                v.wait_ge(s_sc, c)
                v.wait_ge(s_pe, pcnt)
                v.tensor_tensor(out=sqbf[:, :, :], in0=lnA[:, 4:8, :],
                                in1=lnA[:, 4:8, :], op=ALU.mult).then_inc(s_v, 1)

            ve(_sq1)
            C["v"] += 1

            def _sqm1(t_):
                t_.wait_ge(s_v, C["v"])
                for j in range(4):
                    mm = t_.matmul(pg[0:1, 7, :], onesc[:, 0:1], sqbf[:, j, :],
                                   start=False, stop=(j == 3))
                mm.then_inc(s_pe, 1)

            pe(_sqm1)
            C["pe"] += 1
        else:
            ln_cast(t, lnA, deps=(("pe", C["pe"]),))
            ffn_round(nxt, lnbf, 0, 0, v_ln1_n)
            ffn_round(nxt, lnbf, 1, 1, v_ln1_n)
            ln_summ(t, lnA, 6)
            ffn_round(nxt, lnbf, 2, 0, v_ln1_n)
            ffn_round(nxt, lnbf, 3, 1, v_ln1_n)
            ln_sumsq(t, lnA, 7)
        ln_rows(t, (6, 7))
        ln_sqrt(t)
        ln_fin(t)
        v_fin3 = C["v"]
        if nxt is not None:
            ffn_round(nxt, lnbf, 4, 0, v_ln1_n)
            ffn_round(nxt, lnbf, 5, 1, v_ln1_n)
        ln_bcast(t, (6, 7), v_cnt=v_fin3)
        ln_copyrb(t, (6, 7))
        if nxt is not None:
            ffn_round(nxt, lnbf, 6, 0, v_ln1_n)
        ln_apply(t, lnA, (6, 7))
        v_ln3 = C["v"]
        if nxt is not None:
            ffn_round(nxt, lnbf, 7, 1, v_ln1_n)
        if nxt is not None:
            ln_cast(nxt, lnbf, deps=(("pe", C["pe"]),))
        ffn_round(t, lnA, 0, 4, v_ln3)
        ffn_round(t, lnA, 1, 5, v_ln3)
        if nxt is not None:
            ln_summ(nxt, lnbf, 2)
        ffn_round(t, lnA, 2, 4, v_ln3)
        ffn_round(t, lnA, 3, 5, v_ln3)
        if nxt is not None:
            ln_sumsq(nxt, lnbf, 3)
        if nxt is not None:
            ln_rows(nxt, (2, 3))
            ln_sqrt(nxt)
            ln_fin(nxt)
            v_fin2 = C["v"]
        ffn_round(t, lnA, 4, 4, v_ln3)
        ffn_round(t, lnA, 5, 5, v_ln3)
        if nxt is not None:
            ln_bcast(nxt, (2, 3), v_cnt=v_fin2)
            ln_copyrb(nxt, (2, 3))
        ffn_round(t, lnA, 6, 4, v_ln3)
        if nxt is not None:
            ln_apply(nxt, lnbf, (2, 3))
            v_ln2_n = C["v"]
        ffn_round(t, lnA, 7, 5, v_ln3)
        emit_store(t)
        if nxt is not None:
            for m in range(EC):
                win_round(nxt, lnbf, m, (m % 2), v_ln2_n, warm=False)
                if m == 13:
                    sc_wb_head = C["sc"]
            wb_mm(nxt, 2, warm=False)
            scan_stage(nxt, 2, warm=False)
            v_hs = C["v"]
            if nxt + 1 <= NT:
                emit_load(nxt + 1)
            for e in range(EC):
                win_round(nxt, lnbf, EC + e, (e % 2), v_ln2_n, warm=False)
                wc_y_round(nxt, e, 4 + (e % 2), v_hs, sc_z=C["sc"])
        scq_flush()

    def _final(g):
        g.wait_ge(d_out, C["dout"])

    gp(_final)

    es.close()
    return nc


def kernel(L0_attn_x, L0_attn_W_in, L0_attn_b_in, L0_attn_W_B, L0_attn_W_C,
           L0_attn_D, L0_attn_W_out, L0_attn_b_out, L0_ffn_weight):
    x = np.asarray(L0_attn_x)
    winT = np.ascontiguousarray(np.asarray(L0_attn_W_in).T)
    wfT = np.ascontiguousarray(np.asarray(L0_ffn_weight).T)
    wbT = np.ascontiguousarray(np.asarray(L0_attn_W_B).T)
    wcT = np.ascontiguousarray(np.asarray(L0_attn_W_C).T)
    woT = np.ascontiguousarray(np.asarray(L0_attn_W_out).T)
    b_in_pc = np.ascontiguousarray(np.asarray(L0_attn_b_in).reshape(D2 // 128, 128).T)
    b_out_pc = np.ascontiguousarray(np.asarray(L0_attn_b_out).reshape(DC, 128).T)
    d_pc = np.ascontiguousarray(np.asarray(L0_attn_D).reshape(EC, 128).T)

    B, T, _ = x.shape
    CPB = NCORES // B
    CT = T // CPB

    in_maps = []
    for k in range(NCORES):
        b, j = divmod(k, CPB)
        start = j * CT
        xs = np.zeros((D, TC), np.float32)
        xs[:, W:] = x[b, start : start + CT, :].T
        if j > 0:
            xs[:, :W] = x[b, start - W : start, :].T
            mw = np.full((S, W), np.float32(1.0), np.float32)
        else:
            mw = np.zeros((S, W), np.float32)
        blocks = [xs[:, 0:W].reshape(DC, 128, W)]
        for tt in range(NT):
            blocks.append(xs[:, W + tt * TT : W + (tt + 1) * TT].reshape(DC, 128, TT))
        xs = np.concatenate([blk.ravel() for blk in blocks]).reshape(D, TC)
        in_maps.append({
            "xT": np.ascontiguousarray(xs),
            "wfT": wfT, "winT": winT, "wbT": wbT, "wcT": wcT, "woT": woT,
            "b_in_pc": b_in_pc, "b_out_pc": b_out_pc, "d_pc": d_pc,
            "mask_w": mw,
        })

    nc = build_program()
    res = run_bass_kernel_spmd(nc, in_maps, core_ids=list(range(NCORES)))

    out = np.empty((B, T, D), np.float32)
    for k in range(NCORES):
        b, j = divmod(k, CPB)
        blocks = np.asarray(res.results[k]["outT"]).reshape(NT, DC, 128, TT)
        for tt in range(NT):
            out[b, j * CT + tt * TT : j * CT + (tt + 1) * TT, :] = blocks[tt].reshape(D, TT).T
    return out


# version 49 (final)
# speedup vs baseline: 1.0571x; 1.0133x over previous
import contextlib

import numpy as np

import concourse.bass as bass
import concourse.mybir as mybir
from concourse.bass_utils import run_bass_kernel_spmd
from concourse import dve_ops

F32 = mybir.dt.float32
BF16 = mybir.dt.bfloat16
AF = mybir.ActivationFunctionType
ALU = mybir.AluOpType
AX = mybir.AxisListType

D = 1024
DC = 8
D2 = 4096
DI = 2048
EC = 16
S = 16
TT = 512
NT = 4
W = 192
TC = W + NT * TT
NCORES = 8
EPS = 1e-5


def build_program():
    nc = bass.Bass("TRN2", target_bir_lowering=False)

    d_xT = nc.dram_tensor("xT", [D, TC], F32, kind="ExternalInput").ap()
    d_wfT = nc.dram_tensor("wfT", [D, D], F32, kind="ExternalInput").ap()
    d_winT = nc.dram_tensor("winT", [D, D2], F32, kind="ExternalInput").ap()
    d_wbT = nc.dram_tensor("wbT", [DI, S], F32, kind="ExternalInput").ap()
    d_wcT = nc.dram_tensor("wcT", [S, DI], F32, kind="ExternalInput").ap()
    d_woT = nc.dram_tensor("woT", [DI, D], F32, kind="ExternalInput").ap()
    d_bin = nc.dram_tensor("b_in_pc", [128, D2 // 128], F32, kind="ExternalInput").ap()
    d_bout = nc.dram_tensor("b_out_pc", [128, DC], F32, kind="ExternalInput").ap()
    d_dvec = nc.dram_tensor("d_pc", [128, EC], F32, kind="ExternalInput").ap()
    d_mask = nc.dram_tensor("mask_w", [S, W], F32, kind="ExternalInput").ap()
    d_outT = nc.dram_tensor("outT", [D, NT * TT], F32, kind="ExternalOutput").ap()

    es = contextlib.ExitStack()
    sb = lambda name, shape, dt: es.enter_context(nc.sbuf_tensor(name, shape, dt))
    psum = lambda name, shape, dt: es.enter_context(nc.psum_tensor(name, shape, dt))
    sem = lambda name: es.enter_context(nc.semaphore(name))

    xb = sb("xb", [128, 2, DC, TT], F32)
    winT = sb("winT_bf", [128, DC, D2], BF16)
    woT = sb("woT_bf", [128, EC, D], BF16)
    wfT = sb("wfT_bf", [128, DC, D], BF16)
    wbT = sb("wbT_bf", [128, EC, S], BF16)
    wcT = sb("wcT_bf", [S, DI], BF16)
    xz = sb("xz_bf", [128, 2, EC, TT], BF16)
    lnbf = sb("lnbf", [128, DC, TT], BF16)
    scr = sb("scr", [128, 1, TT], F32)
    rows = sb("rows", [1, 2, TT], F32)
    rows_bf = sb("rows_bf", [1, 2, TT], BF16)
    rbm_bf = sb("rbm_bf", [128, 2, TT], BF16)
    constz = sb("constz", [128, 1], F32)
    b_in = sb("b_in", [128, D2 // 128], F32)
    b_out = sb("b_out", [128, DC], F32)
    dvec = sb("dvec", [128, EC], F32)
    maskw = sb("maskw", [S, W], BF16)
    bsc = sb("bsc", [S, TT], F32)
    hsbf = sb("hs_bf", [S, TT], BF16)
    carry = sb("carry", [S, 1], F32)
    c09 = sb("c09", [S, 1], F32)
    c01 = sb("c01", [S, 1], F32)
    onesr_bf = sb("onesr_bf", [1, 128], BF16)
    onesr = sb("onesr", [1, 128], F32)
    onesc = sb("onesc", [128, 1], BF16)
    onescf = sb("onescf", [128, 1], F32)
    epsap = sb("epsap", [1, 1], F32)
    absc = sb("absc", [128, 2], F32)
    sc05 = sb("sc05", [128, 1], F32)
    rsc = sb("rsc", [128, 1], F32)
    srow = sb("srow", [1, 8], F32)

    pg = psum("pg", [128, 8, TT], F32)

    sqbf = sb("sqbf", [128, 4, TT], BF16)
    xpbf = xz[:, 0, :, :]
    zsbf = xz[:, 1, :, :]
    lnA = xz[:, 0, 0:DC, :]
    wfA = xb[:, 1, :, :].rearrange("p c t -> p (c t)")
    wfB = xb[:, 0, :, 256:512]
    wfC = xb[:, 0, :, 0:256]

    s_pe = sem("s_pe")
    s_v = sem("s_v")
    s_sc = sem("s_sc")
    d_s = sem("d_s")
    d_wg = sem("d_wg")
    d_mk = sem("d_mk")
    d_wo = sem("d_wo")
    d_wi = [sem(f"d_wi{j}") for j in range(8)]
    d_out = sem("d_out")
    d_xt = [sem(f"d_x{t}") for t in range(NT + 1)]

    C = {"pe": 0, "v": 0, "sc": 0, "dout": 0}
    SEMS = {"pe": None, "v": None, "sc": None}

    blk = es.enter_context(nc.Block())
    pe, ve, sc, gp, sy = blk.tensor, blk.vector, blk.scalar, blk.gpsimd, blk.sync

    def _sem(kind):
        return {"pe": s_pe, "v": s_v, "sc": s_sc}[kind]

    scq_items = []

    def scq_add(waits, fn):
        C["sc"] += 1
        scq_items.append((list(waits), fn))
        return C["sc"]

    def scq_flush():
        if not scq_items:
            return
        items = list(scq_items)
        del scq_items[:]

        def _sec(s):
            for waits, fn in items:
                for kind, cnt in waits:
                    s.wait_ge(_sem(kind) if isinstance(kind, str) else kind, cnt)
                fn(s).then_inc(s_sc, 1)

        sc(_sec)

    bank_free = {b: ("v", 0) for b in range(8)}

    def wait_banks(eng, banks):
        need = {}
        for b in banks:
            kind, cnt = bank_free[b]
            if cnt > 0:
                need[kind] = max(need.get(kind, 0), cnt)
        for kind, cnt in need.items():
            eng.wait_ge(_sem(kind), cnt)

    def _loads(sync):
        for c in range(4):
            sync.dma_start(out=xb[:, 1, 2 * c : 2 * c + 2, :],
                           in_=d_wfT[128 * c : 128 * c + 128, :].rearrange(
                               "p (a t) -> p a t", t=512)).then_inc(d_s, 16)

    sy(_loads)

    def _loads2(s):
        for c in range(2):
            s.dma_start(out=xb[:, 0, 4 * c : 4 * c + 4, 0:256],
                        in_=d_wfT[768 + 128 * c : 896 + 128 * c, :].rearrange(
                            "p (a t) -> p a t", t=256)).then_inc(d_s, 16)
        for c in range(2):
            s.dma_start(out=xb[:, 0, 4 * c : 4 * c + 4, 256:512],
                        in_=d_wfT[512 + 128 * c : 640 + 128 * c, :].rearrange(
                            "p (a t) -> p a t", t=256)).then_inc(d_s, 16)

    sc(_loads2)

    def _casts(g):
        g.dma_start(out=maskw[:], in_=d_mask).then_inc(d_mk, 16)
        g.dma_start(out=b_in[:], in_=d_bin).then_inc(d_wg, 16)
        g.dma_start(out=b_out[:], in_=d_bout).then_inc(d_wg, 16)
        g.dma_start(out=dvec[:], in_=d_dvec).then_inc(d_wg, 16)
        g.dma_start(out=wbT[:], in_=d_wbT.rearrange("(c p) s -> p c s", p=128)).then_inc(d_wg, 16)
        g.dma_start(out=wcT[:], in_=d_wcT).then_inc(d_wg, 16)
        g.wait_ge(d_s, 8 * 16)
        for j in range(8):
            if j == 2:
                g.wait_ge(d_xt[1], 16)
            g.dma_start(out=winT[:, :, 512 * j : 512 * (j + 1)],
                        in_=d_winT.rearrange("(c p) o -> p c o", p=128)[:, :, 512 * j : 512 * (j + 1)]
                        ).then_inc(d_wi[j], 16)
        g.dma_start(out=woT[:], in_=d_woT.rearrange("(c p) o -> p c o", p=128)).then_inc(d_wo, 16)

    gp(_casts)

    def _consts(v):
        v.memset(onesr[:], 1.0)
        v.memset(onesr_bf[:], 1.0)
        v.memset(onesc[:], 1.0)
        v.memset(onescf[:], 1.0)
        v.memset(epsap[:], EPS)
        v.memset(constz[:], 0.0)
        v.memset(carry[:], 0.0)
        v.memset(c09[:], 9.0)
        v.memset(absc[:], 0.0)
        v.memset(c01[:], 10.0).then_inc(s_v, 1)

    ve(_consts)
    C["v"] += 1

    def _cdiv(v):
        v.wait_ge(s_v, C["v"])
        v.reciprocal(out=c01[:], in_=c01[:]).then_inc(s_v, 1)
        v.wait_ge(s_v, C["v"] + 1)
        v.tensor_scalar(out=c09[:], in0=c01[:], scalar1=9.0, scalar2=None, op0=ALU.mult).then_inc(s_v, 1)

    ve(_cdiv)
    C["v"] += 2

    zq = xz[:, 1, :, :].rearrange("p c t -> p (c t)")

    wfBC = xb[:, 0, :, :].rearrange("p c t -> p (c t)")

    def _abssum(s, vw=C["v"]):
        s.wait_ge(s_v, vw)
        s.wait_ge(d_s, 8 * 16)
        s.activation(out=zq[:, 0:4096], in_=wfA, func=AF.Abs, bias=0.0,
                     scale=1.0, accum_out=absc[:, 0:1])
        s.activation(out=zq[:, 4096:8192], in_=wfBC, func=AF.Abs, bias=0.0,
                     scale=1.0, accum_out=absc[:, 1:2]).then_inc(s_sc, 1)

    sc(_abssum)
    C["sc"] += 1

    def _absmm(t):
        t.wait_ge(s_sc, C["sc"])
        t.wait_ge(s_v, C["v"])
        t.matmul(pg[0:1, 5, 0:2], onescf[:, 0:1], absc[:, 0:2], start=True, stop=True).then_inc(s_pe, 1)

    pe(_absmm)
    C["pe"] += 1

    def _scale(v):
        v.wait_ge(s_pe, C["pe"])
        v.tensor_copy(out=srow[0:1, 0:2], in_=pg[0:1, 5, 0:2]).then_inc(s_v, 1)
        v.wait_ge(s_v, C["v"] + 1)
        v.tensor_tensor(out=srow[0:1, 0:1], in0=srow[0:1, 0:1], in1=srow[0:1, 1:2], op=ALU.add).then_inc(s_v, 1)
        v.wait_ge(s_v, C["v"] + 2)
        v.tensor_scalar(out=srow[0:1, 2:3], in0=srow[0:1, 0:1], scalar1=1.0 / (D * D), scalar2=None, op0=ALU.mult).then_inc(s_v, 1)
        v.wait_ge(s_v, C["v"] + 3)
        v.tensor_scalar(out=srow[0:1, 3:4], in0=srow[0:1, 2:3], scalar1=EPS, scalar2=0.5, op0=ALU.max, op1=ALU.mult).then_inc(s_v, 1)
        v.wait_ge(s_v, C["v"] + 4)
        v.tensor_scalar(out=srow[0:1, 4:5], in0=srow[0:1, 3:4], scalar1=-1.0, scalar2=None, op0=ALU.mult).then_inc(s_v, 1)

    ve(_scale)
    C["v"] += 5

    def _scalebc(t):
        t.wait_ge(s_v, C["v"])
        t.matmul(pg[:, 4, 0:1], onesr[0:1, :], srow[0:1, 3:4], start=True, stop=True)
        mm = t.matmul(pg[:, 4, 1:2], onesr[0:1, :], srow[0:1, 4:5], start=True, stop=True)
        mm.then_inc(s_pe, 1)

    pe(_scalebc)
    C["pe"] += 1

    def _scalecp(v):
        v.wait_ge(s_pe, C["pe"])
        v.tensor_copy(out=sc05[:], in_=pg[:, 4, 0:1])
        v.tensor_copy(out=rsc[:], in_=pg[:, 4, 1:2]).then_inc(s_v, 1)

    ve(_scalecp)
    C["v"] += 1
    bank_free[4] = ("v", C["v"])
    bank_free[5] = ("v", C["v"])

    wfT_flat = wfT[:, :, :].rearrange("p c o -> p (c o)")
    QV = {}

    def _quant(v):
        n = 0
        for name, qsrc, lo, hi in (("C", wfC, 6144, 8192), ("A", wfA, 0, 4096),
                                   ("B", wfB, 4096, 6144)):
            dst = wfT_flat[:, lo:hi]
            tmp = zq[:, 0 : hi - lo]
            if name in ("B", "C"):
                dst = dst.rearrange("p (a t) -> p a t", t=256)
                tmp = tmp.rearrange("p (a t) -> p a t", t=256)
            v.wait_ge(s_v, C["v"] + n)
            v.tensor_scalar(out=tmp, in0=qsrc, scalar1=rsc[:, 0:1], scalar2=None, op0=ALU.is_le).then_inc(s_v, 1)
            v.tensor_scalar(out=dst, in0=qsrc, scalar1=sc05[:, 0:1], scalar2=None, op0=ALU.is_ge).then_inc(s_v, 1)
            QV[name] = C["v"] + n + 2
            v.wait_ge(s_v, C["v"] + n + 2)
            v.tensor_tensor(out=dst, in0=dst, in1=tmp, op=ALU.subtract).then_inc(s_v, 1)
            n += 3

    def x_of(tile):
        tw = W if tile == 0 else TT
        return xb[:, tile % 2, :, 0:tw], tw

    rowA = rows[0:1, 0, :]
    rowB = rows[0:1, 1, :]
    rowS = scr[0:1, 0, :]

    def ln_cast(tile, buf, deps=(), vdep=None):
        xsrc, tw = x_of(tile)
        vw = [] if vdep == 0 else [("v", C["v"] if vdep is None else vdep)]
        scq_add(vw + list(deps),
                lambda s, buf=buf, xsrc=xsrc, tw=tw: s.activation(
                    out=buf[:, :, 0:tw], in_=xsrc, func=AF.Copy, bias=0.0, scale=1.0))

    def ln_summ(tile, buf, bank):
        _, tw = x_of(tile)

        def _f(t, sc_cnt=C["sc"]):
            t.wait_ge(s_sc, sc_cnt)
            wait_banks(t, [bank])
            for kc in range(DC):
                mm = t.matmul(pg[0:1, bank, 0:tw], onesc[:, 0:1], buf[:, kc, 0:tw],
                              start=(kc == 0), stop=(kc == DC - 1))
            mm.then_inc(s_pe, 1)

        pe(_f)
        C["pe"] += 1

    def ln_sumsq(tile, buf, bank):
        _, tw = x_of(tile)
        for q in range(2):
            def _sq(v, q=q, sc_cnt=C["sc"], pe_cnt=C["pe"]):
                if q == 0:
                    v.wait_ge(s_sc, sc_cnt)
                else:
                    v.wait_ge(s_pe, pe_cnt)
                v.tensor_tensor(out=sqbf[:, :, 0:tw], in0=buf[:, 4 * q : 4 * q + 4, 0:tw],
                                in1=buf[:, 4 * q : 4 * q + 4, 0:tw], op=ALU.mult).then_inc(s_v, 1)

            ve(_sq)
            C["v"] += 1

            def _mm(t, q=q):
                t.wait_ge(s_v, C["v"])
                if q == 0:
                    wait_banks(t, [bank])
                for j in range(4):
                    mm = t.matmul(pg[0:1, bank, 0:tw], onesc[:, 0:1], sqbf[:, j, 0:tw],
                                  start=(q == 0 and j == 0), stop=(q == 1 and j == 3))
                mm.then_inc(s_pe, 1)

            pe(_mm)
            C["pe"] += 1

    def ln_rows(tile, banks):
        _, tw = x_of(tile)
        b0, b1 = banks

        def _f(v):
            v.wait_ge(s_pe, C["pe"])
            v.tensor_scalar(out=rows[0:1, :, 0:tw], in0=pg[0:1, b0 : b0 + 2, 0:tw],
                            scalar1=1.0 / D, scalar2=None, op0=ALU.mult).then_inc(s_v, 1)
            v.wait_ge(s_v, C["v"] + 1)
            v.tensor_tensor(out=rowS[:, 0:tw], in0=rowA[:, 0:tw], in1=rowA[:, 0:tw], op=ALU.mult).then_inc(s_v, 1)
            v.wait_ge(s_v, C["v"] + 2)
            v.tensor_tensor(out=rowB[:, 0:tw], in0=rowB[:, 0:tw], in1=rowS[:, 0:tw], op=ALU.subtract).then_inc(s_v, 1)

        ve(_f)
        C["v"] += 3

    def ln_sqrt(tile):
        _, tw = x_of(tile)
        scq_add([("v", C["v"])],
                lambda s, tw=tw: s.activation(out=rowB[:, 0:tw], in_=rowB[:, 0:tw],
                                              func=AF.Ln, bias=epsap[0:1, 0:1], scale=1.0))
        scq_add([("sc", C["sc"])],
                lambda s, tw=tw: s.activation(out=rowB[:, 0:tw], in_=rowB[:, 0:tw],
                                              func=AF.Exp, bias=0.0, scale=-0.5))

    def ln_fin(tile):
        _, tw = x_of(tile)

        def _f(v):
            v.wait_ge(s_sc, C["sc"])
            v.tensor_tensor(out=rowA[:, 0:tw], in0=rowA[:, 0:tw], in1=rowB[:, 0:tw], op=ALU.mult).then_inc(s_v, 1)
            v.wait_ge(s_v, C["v"] + 1)
            v.tensor_copy(out=rows_bf[0:1, :, 0:tw], in_=rows[0:1, :, 0:tw]).then_inc(s_v, 1)

        ve(_f)
        C["v"] += 2

    def ln_bcast(tile, banks, v_cnt=None):
        _, tw = x_of(tile)
        b0, b1 = banks
        vw = C["v"] if v_cnt is None else v_cnt

        def _f(t):
            t.wait_ge(s_v, vw)
            wait_banks(t, [b0, b1])
            t.matmul(pg[:, b0, 0:tw], onesr_bf[0:1, :], rows_bf[0:1, 1, 0:tw], start=True, stop=True)
            mm = t.matmul(pg[:, b1, 0:tw], onesr_bf[0:1, :], rows_bf[0:1, 0, 0:tw], start=True, stop=True)
            mm.then_inc(s_pe, 1)

        pe(_f)
        C["pe"] += 1

    def ln_copyrb(tile, banks):
        _, tw = x_of(tile)
        b0, b1 = banks
        scq_add([("pe", C["pe"])],
                lambda s, tw=tw, b0=b0: s.activation(out=rbm_bf[:, 0, 0:tw], in_=pg[:, b0, 0:tw],
                                                     func=AF.Copy, bias=0.0, scale=1.0))
        cnt = scq_add([],
                      lambda s, tw=tw, b1=b1: s.activation(out=rbm_bf[:, 1, 0:tw], in_=pg[:, b1, 0:tw],
                                                           func=AF.Copy, bias=0.0, scale=1.0))
        bank_free[b0] = ("sc", cnt)
        bank_free[b1] = ("sc", cnt)

    def ln_apply(tile, buf, banks):
        _, tw = x_of(tile)
        base = C["v"]

        def _f(v):
            v.wait_ge(s_sc, C["sc"])
            v.tensor_tensor(out=buf[:, :, 0:tw], in0=buf[:, :, 0:tw],
                            in1=rbm_bf[:, 0:1, 0:tw].broadcast_to([128, DC, tw]),
                            op=ALU.mult).then_inc(s_v, 1)
            v.wait_ge(s_v, base + 1)
            v.tensor_tensor(out=buf[:, :, 0:tw], in0=buf[:, :, 0:tw],
                            in1=rbm_bf[:, 1:2, 0:tw].broadcast_to([128, DC, tw]),
                            op=ALU.subtract).then_inc(s_v, 1)

        ve(_f)
        C["v"] += 2

    def ffn_round(tile, buf, m, bank, v_ready):
        xsrc, tw = x_of(tile)

        def _mm(t):
            t.wait_ge(s_v, v_ready)
            wait_banks(t, [bank])
            for kc in range(DC):
                mm = t.matmul(pg[:, bank, 0:tw], wfT[:, kc, m * 128 : (m + 1) * 128],
                              buf[:, kc, 0:tw], start=(kc == 0), stop=(kc == DC - 1))
            mm.then_inc(s_pe, 1)

        pe(_mm)
        C["pe"] += 1

        def _evac(v):
            v.wait_ge(s_pe, C["pe"])
            v.scalar_tensor_tensor(out=xsrc[:, m, :], in0=pg[:, bank, 0:tw],
                                   scalar=sc05[:, 0:1], in1=xsrc[:, m, :],
                                   op0=ALU.mult, op1=ALU.add).then_inc(s_v, 1)

        ve(_evac)
        C["v"] += 1
        bank_free[bank] = ("v", C["v"])

    def win_round(tile, buf, m, bank, v_ready, warm):
        _, tw = x_of(tile)

        def _mm(t):
            t.wait_ge(d_wi[m // 4], 16)
            t.wait_ge(s_v, v_ready)
            wait_banks(t, [bank])
            for kc in range(DC):
                mm = t.matmul(pg[:, bank, 0:tw], winT[:, kc, m * 128 : (m + 1) * 128],
                              buf[:, kc, 0:tw], start=(kc == 0), stop=(kc == DC - 1))
            mm.then_inc(s_pe, 1)

        pe(_mm)
        C["pe"] += 1

        if warm:
            out = zsbf[:, m, 0:tw] if m < DC else xpbf[:, m, 0:tw]
        else:
            out = xpbf[:, m, 0:tw] if m < EC else zsbf[:, m - EC, 0:tw]
        cnt = scq_add([("pe", C["pe"])],
                      lambda s, out=out, bank=bank, m=m, tw=tw: s.activation(
                          out=out, in_=pg[:, bank, 0:tw], func=AF.Silu,
                          bias=b_in[:, m : m + 1], scale=1.0))
        bank_free[bank] = ("sc", cnt)

    def wb_mm(tile, bank, warm, head_sc=None):
        _, tw = x_of(tile)

        def _rhs(kc):
            if warm:
                return zsbf[:, kc, 0:tw] if kc < DC else xpbf[:, kc, 0:tw]
            return xpbf[:, kc, 0:tw]

        def _head(t, hs=head_sc if head_sc is not None else C["sc"]):
            t.wait_ge(s_sc, hs)
            wait_banks(t, [bank])
            for kc in range(EC - 2):
                t.matmul(pg[0:16, bank, 0:tw], wbT[:, kc, :], _rhs(kc),
                         start=(kc == 0), stop=False)

        pe(_head)

        def _tail(t):
            t.wait_ge(s_sc, C["sc"])
            for kc in range(EC - 2, EC):
                mm = t.matmul(pg[0:16, bank, 0:tw], wbT[:, kc, :], _rhs(kc),
                              start=False, stop=(kc == EC - 1))
            mm.then_inc(s_pe, 1)

        pe(_tail)
        C["pe"] += 1

    def scan_stage(tile, bank, warm):
        _, tw = x_of(tile)
        base = C["v"]

        def _f(v):
            v.wait_ge(s_pe, C["pe"])
            if warm:
                v.wait_ge(d_mk, 16)
                v.scalar_tensor_tensor(out=bsc[:, 0:tw], in0=pg[0:16, bank, 0:tw],
                                       scalar=c01[:, 0:1], in1=maskw[:, 0:tw],
                                       op0=ALU.mult, op1=ALU.mult).then_inc(s_v, 1)
            else:
                v.tensor_scalar(out=bsc[:, 0:tw], in0=pg[0:16, bank, 0:tw],
                                scalar1=0.1, scalar2=None, op0=ALU.mult).then_inc(s_v, 1)
            v.wait_ge(s_v, base + 1)
            v.tensor_tensor_scan(out=bsc[:, 0:tw], data0=c09[:, 0:1].broadcast_to([S, tw]),
                                 data1=bsc[:, 0:tw], initial=carry[:, 0:1],
                                 op0=ALU.mult, op1=ALU.add).then_inc(s_v, 1)
            v.wait_ge(s_v, base + 2)
            v.tensor_copy(out=carry[:, 0:1], in_=bsc[:, tw - 1 : tw])
            v.tensor_copy(out=hsbf[:, 0:tw], in_=bsc[:, 0:tw]).then_inc(s_v, 1)

        ve(_f)
        C["v"] += 3
        bank_free[bank] = ("v", C["v"])

    def wc_y_round(tile, e, bank, v_hs, sc_z=None):
        _, tw = x_of(tile)

        def _mm(t):
            t.wait_ge(s_v, v_hs)
            wait_banks(t, [bank])
            t.matmul(pg[:, bank, 0:tw], wcT[:, e * 128 : (e + 1) * 128],
                     hsbf[:, 0:tw], start=True, stop=True).then_inc(s_pe, 1)

        pe(_mm)
        C["pe"] += 1
        base = C["v"]

        def _yev(v):
            v.wait_ge(s_pe, C["pe"])
            v.wait_ge(s_v, base)
            if sc_z is not None:
                v.wait_ge(s_sc, sc_z)
            v.scalar_tensor_tensor(out=scr[:, 0, 0:tw], in0=xpbf[:, e, 0:tw],
                                   scalar=dvec[:, e : e + 1], in1=pg[:, bank, 0:tw],
                                   op0=ALU.mult, op1=ALU.add).then_inc(s_v, 1)
            v.wait_ge(s_v, base + 1)
            v.tensor_tensor(out=zsbf[:, e, 0:tw], in0=scr[:, 0, 0:tw],
                            in1=zsbf[:, e, 0:tw], op=ALU.mult).then_inc(s_v, 1)

        ve(_yev)
        C["v"] += 2
        bank_free[bank] = ("v", C["v"])

    def wout_round(tile, m, bank, v_ready):
        xsrc, tw = x_of(tile)

        def _mm(t):
            t.wait_ge(s_v, v_ready)
            wait_banks(t, [bank])
            for kc in range(EC):
                mm = t.matmul(pg[:, bank, 0:tw], woT[:, kc, m * 128 : (m + 1) * 128],
                              zsbf[:, kc, 0:tw], start=(kc == 0), stop=(kc == EC - 1))
            mm.then_inc(s_pe, 1)

        pe(_mm)
        C["pe"] += 1

        def _evac(v):
            v.wait_ge(s_pe, C["pe"])
            v.scalar_tensor_tensor(out=xsrc[:, m, :], in0=pg[:, bank, 0:tw],
                                   scalar=b_out[:, m : m + 1], in1=xsrc[:, m, :],
                                   op0=ALU.add, op1=ALU.add).then_inc(s_v, 1)

        ve(_evac)
        C["v"] += 1
        bank_free[bank] = ("v", C["v"])

    def ffn2_round(tile, buf, m, bank, v_ready):
        ffn_round(tile, buf, m, bank, v_ready)

    loads_issued = {}

    xT_flat = d_xT.rearrange("d t -> (d t)")
    outT_flat = d_outT.rearrange("d t -> (d t)")

    def emit_load(tile, extra_v_wait=None):
        warm = tile == 0
        par = tile % 2
        tw = W if warm else TT
        base = 0 if warm else D * (W + (tile - 1) * TT)

        def _load(sync, par=par, tw=tw, base=base, tile=tile, ev=extra_v_wait):
            if ev is not None:
                sync.wait_ge(s_v, ev)
            if tile >= 3:
                sync.wait_ge(d_out, 16 * DC * (tile - 2))
            sync.dma_start(out=xb[:, par, :, 0:tw],
                           in_=xT_flat[base : base + D * tw].rearrange(
                               "(c p t) -> p c t", p=128, t=tw)).then_inc(d_xt[tile], 16)

        sy(_load)
        loads_issued[tile] = True

    def emit_store(tile, ev):
        def _store(sync, par=tile % 2, tile=tile, ev=tuple(ev)):
            base = (tile - 1) * D * TT
            for m in range(DC):
                sync.wait_ge(s_v, ev[m])
                sync.dma_start(out=outT_flat[base + m * 128 * TT : base + (m + 1) * 128 * TT].rearrange(
                                   "(p t) -> p t", t=TT),
                               in_=xb[:, par, m, :]).then_inc(d_out, 16)

        sy(_store)
        C["dout"] += 16 * DC

    def wait_x(tile, extra_wg=False):
        def _f(v, tile=tile, wg=extra_wg):
            v.wait_ge(d_xt[tile], 16)
            if wg:
                v.wait_ge(d_wg, 64)

        ve(_f)


    def front_ln1(tile, buf, sbanks, cast_deps=(), cast_vdep=None):
        ln_cast(tile, buf, deps=cast_deps, vdep=cast_vdep)
        ln_summ(tile, buf, sbanks[0])
        ln_sumsq(tile, buf, sbanks[1])
        ln_rows(tile, sbanks)
        ln_sqrt(tile)
        ln_fin(tile)
        ln_bcast(tile, sbanks)
        ln_copyrb(tile, sbanks)
        ln_apply(tile, buf, sbanks)

    ve(_quant)
    C["v"] += 9
    v_quant = C["v"]
    emit_load(0, extra_v_wait=QV["C"])
    emit_load(1, extra_v_wait=QV["A"])
    wait_x(0)
    front_ln1(0, lnbf, (6, 7), cast_deps=((d_xt[0], 16),), cast_vdep=0)
    v_warm_ln1 = C["v"]
    for m in range(DC):
        ffn_round(0, lnbf, m, 4 + (m % 2), v_warm_ln1)
    front_ln1(0, lnbf, (6, 7), cast_deps=(("pe", C["pe"]),))
    v_warm_ln2 = C["v"]
    wait_x(1)
    front_ln1(1, lnA, (2, 3), cast_deps=((d_xt[1], 16),))
    v_f1_ln1 = C["v"]

    def _wg48(t):
        t.wait_ge(d_wg, 80)

    pe(_wg48)
    for m in range(EC):
        win_round(0, lnbf, m, 4 + (m % 2), v_warm_ln2, warm=True)
        if m == 13:
            sc_warm_wb_head = C["sc"]
    wb_mm(0, 6, warm=True, head_sc=sc_warm_wb_head)
    scan_stage(0, 6, warm=True)
    for m in range(DC):
        ffn_round(1, lnA, m, (m % 2), v_f1_ln1)
    front_ln1(1, lnbf, (2, 3), cast_deps=(("pe", C["pe"]),))
    v_f1_ln2 = C["v"]
    for m in range(EC):
        win_round(1, lnbf, m, (m % 2), v_f1_ln2, warm=False)
        if m == 13:
            sc_f1_wb_head = C["sc"]
    wb_mm(1, 2, warm=False, head_sc=sc_f1_wb_head)
    scan_stage(1, 2, warm=False)
    v_hs = C["v"]
    emit_load(2, extra_v_wait=C["v"])
    for e in range(EC):
        win_round(1, lnbf, EC + e, (e % 2), v_f1_ln2, warm=False)
        wc_y_round(1, e, 4 + (e % 2), v_hs, sc_z=C["sc"])
    scq_flush()

    def _wg64(t):
        t.wait_ge(d_wo, 16)

    pe(_wg64)

    for t in range(1, NT + 1):
        nxt = t + 1 if t < NT else None
        y_done = C["v"]
        if nxt is not None:
            ln_cast(nxt, lnbf, deps=(("pe", C["pe"]), (d_xt[nxt], 16)))
        wout_round(t, 0, 4, y_done)
        wout_round(t, 1, 5, y_done)
        if nxt is not None:
            ln_summ(nxt, lnbf, 2)
        wout_round(t, 2, 6, y_done)
        wout_round(t, 3, 7, y_done)
        v_w3 = C["v"]
        if nxt is not None:
            ln_sumsq(nxt, lnbf, 3)
        if nxt is not None:
            ln_rows(nxt, (2, 3))
            ln_sqrt(nxt)
            ln_fin(nxt)
            v_fin1 = C["v"]
        wout_round(t, 4, 4, y_done)
        wout_round(t, 5, 5, y_done)
        if nxt is not None:
            ln_bcast(nxt, (2, 3), v_cnt=v_fin1)
            ln_copyrb(nxt, (2, 3))
        wout_round(t, 6, 6, y_done)
        if nxt is not None:
            ln_apply(nxt, lnbf, (2, 3))
            v_ln1_n = C["v"]
        wout_round(t, 7, 7, y_done)
        v_w7 = C["v"]
        if nxt is None:
            par = t % 2
            cq0 = scq_add([("v", v_w3)],
                          lambda s, par=par: s.activation(out=lnA[:, 0:4, :], in_=xb[:, par, 0:4, :],
                                                          func=AF.Copy, bias=0.0, scale=1.0))
            cq1 = scq_add([("v", v_w7)],
                          lambda s, par=par: s.activation(out=lnA[:, 4:8, :], in_=xb[:, par, 4:8, :],
                                                          func=AF.Copy, bias=0.0, scale=1.0))

            def _sm0(t_, c=cq0):
                t_.wait_ge(s_sc, c)
                wait_banks(t_, [6])
                for kc in range(4):
                    t_.matmul(pg[0:1, 6, :], onesc[:, 0:1], lnA[:, kc, :],
                              start=(kc == 0), stop=False)

            pe(_sm0)

            def _sq0(v, c=cq0):
                v.wait_ge(s_sc, c)
                v.tensor_tensor(out=sqbf[:, :, :], in0=lnA[:, 0:4, :],
                                in1=lnA[:, 0:4, :], op=ALU.mult).then_inc(s_v, 1)

            ve(_sq0)
            C["v"] += 1

            def _sqm0(t_):
                t_.wait_ge(s_v, C["v"])
                wait_banks(t_, [7])
                for j in range(4):
                    mm = t_.matmul(pg[0:1, 7, :], onesc[:, 0:1], sqbf[:, j, :],
                                   start=(j == 0), stop=False)
                mm.then_inc(s_pe, 1)

            pe(_sqm0)
            C["pe"] += 1

            def _sm1(t_, c=cq1):
                t_.wait_ge(s_sc, c)
                for kc in range(4, 8):
                    mm = t_.matmul(pg[0:1, 6, :], onesc[:, 0:1], lnA[:, kc, :],
                                   start=False, stop=(kc == 7))
                mm.then_inc(s_pe, 1)

            pe(_sm1)
            C["pe"] += 1

            def _sq1(v, c=cq1, pcnt=C["pe"]):
                v.wait_ge(s_sc, c)
                v.wait_ge(s_pe, pcnt)
                v.tensor_tensor(out=sqbf[:, :, :], in0=lnA[:, 4:8, :],
                                in1=lnA[:, 4:8, :], op=ALU.mult).then_inc(s_v, 1)

            ve(_sq1)
            C["v"] += 1

            def _sqm1(t_):
                t_.wait_ge(s_v, C["v"])
                for j in range(4):
                    mm = t_.matmul(pg[0:1, 7, :], onesc[:, 0:1], sqbf[:, j, :],
                                   start=False, stop=(j == 3))
                mm.then_inc(s_pe, 1)

            pe(_sqm1)
            C["pe"] += 1
        else:
            ln_cast(t, lnA, deps=(("pe", C["pe"]),))
            ffn_round(nxt, lnbf, 0, 0, v_ln1_n)
            ffn_round(nxt, lnbf, 1, 1, v_ln1_n)
            ln_summ(t, lnA, 6)
            ffn_round(nxt, lnbf, 2, 0, v_ln1_n)
            ffn_round(nxt, lnbf, 3, 1, v_ln1_n)
            ln_sumsq(t, lnA, 7)
        ln_rows(t, (6, 7))
        ln_sqrt(t)
        ln_fin(t)
        v_fin3 = C["v"]
        if nxt is not None:
            ffn_round(nxt, lnbf, 4, 0, v_ln1_n)
            ffn_round(nxt, lnbf, 5, 1, v_ln1_n)
        ln_bcast(t, (6, 7), v_cnt=v_fin3)
        ln_copyrb(t, (6, 7))
        if nxt is not None:
            ffn_round(nxt, lnbf, 6, 0, v_ln1_n)
        ln_apply(t, lnA, (6, 7))
        v_ln3 = C["v"]
        if nxt is not None:
            ffn_round(nxt, lnbf, 7, 1, v_ln1_n)
        if nxt is not None:
            ln_cast(nxt, lnbf, deps=(("pe", C["pe"]),))
        ev = []
        ffn_round(t, lnA, 0, 4, v_ln3)
        ev.append(C["v"])
        ffn_round(t, lnA, 1, 5, v_ln3)
        ev.append(C["v"])
        if nxt is not None:
            ln_summ(nxt, lnbf, 2)
        ffn_round(t, lnA, 2, 4, v_ln3)
        ev.append(C["v"])
        ffn_round(t, lnA, 3, 5, v_ln3)
        ev.append(C["v"])
        if nxt is not None:
            ln_sumsq(nxt, lnbf, 3)
        if nxt is not None:
            ln_rows(nxt, (2, 3))
            ln_sqrt(nxt)
            ln_fin(nxt)
            v_fin2 = C["v"]
        ffn_round(t, lnA, 4, 4, v_ln3)
        ev.append(C["v"])
        ffn_round(t, lnA, 5, 5, v_ln3)
        ev.append(C["v"])
        if nxt is not None:
            ln_bcast(nxt, (2, 3), v_cnt=v_fin2)
            ln_copyrb(nxt, (2, 3))
        ffn_round(t, lnA, 6, 4, v_ln3)
        ev.append(C["v"])
        if nxt is not None:
            ln_apply(nxt, lnbf, (2, 3))
            v_ln2_n = C["v"]
        ffn_round(t, lnA, 7, 5, v_ln3)
        ev.append(C["v"])
        emit_store(t, ev)
        if nxt is not None:
            for m in range(EC):
                win_round(nxt, lnbf, m, (m % 2), v_ln2_n, warm=False)
                if m == 13:
                    sc_wb_head = C["sc"]
            wb_mm(nxt, 2, warm=False)
            scan_stage(nxt, 2, warm=False)
            v_hs = C["v"]
            if nxt + 1 <= NT:
                emit_load(nxt + 1)
            for e in range(EC):
                win_round(nxt, lnbf, EC + e, (e % 2), v_ln2_n, warm=False)
                wc_y_round(nxt, e, 4 + (e % 2), v_hs, sc_z=C["sc"])
        scq_flush()

    def _final(g):
        g.wait_ge(d_out, C["dout"])

    gp(_final)

    es.close()
    return nc


def kernel(L0_attn_x, L0_attn_W_in, L0_attn_b_in, L0_attn_W_B, L0_attn_W_C,
           L0_attn_D, L0_attn_W_out, L0_attn_b_out, L0_ffn_weight):
    x = np.asarray(L0_attn_x)
    winT = np.ascontiguousarray(np.asarray(L0_attn_W_in).T)
    wfT = np.ascontiguousarray(np.asarray(L0_ffn_weight).T)
    wbT = np.ascontiguousarray(np.asarray(L0_attn_W_B).T)
    wcT = np.ascontiguousarray(np.asarray(L0_attn_W_C).T)
    woT = np.ascontiguousarray(np.asarray(L0_attn_W_out).T)
    b_in_pc = np.ascontiguousarray(np.asarray(L0_attn_b_in).reshape(D2 // 128, 128).T)
    b_out_pc = np.ascontiguousarray(np.asarray(L0_attn_b_out).reshape(DC, 128).T)
    d_pc = np.ascontiguousarray(np.asarray(L0_attn_D).reshape(EC, 128).T)

    B, T, _ = x.shape
    CPB = NCORES // B
    CT = T // CPB

    in_maps = []
    for k in range(NCORES):
        b, j = divmod(k, CPB)
        start = j * CT
        xs = np.zeros((D, TC), np.float32)
        xs[:, W:] = x[b, start : start + CT, :].T
        if j > 0:
            xs[:, :W] = x[b, start - W : start, :].T
            mw = np.full((S, W), np.float32(1.0), np.float32)
        else:
            mw = np.zeros((S, W), np.float32)
        blocks = [xs[:, 0:W].reshape(DC, 128, W)]
        for tt in range(NT):
            blocks.append(xs[:, W + tt * TT : W + (tt + 1) * TT].reshape(DC, 128, TT))
        xs = np.concatenate([blk.ravel() for blk in blocks]).reshape(D, TC)
        in_maps.append({
            "xT": np.ascontiguousarray(xs),
            "wfT": wfT, "winT": winT, "wbT": wbT, "wcT": wcT, "woT": woT,
            "b_in_pc": b_in_pc, "b_out_pc": b_out_pc, "d_pc": d_pc,
            "mask_w": mw,
        })

    nc = build_program()
    res = run_bass_kernel_spmd(nc, in_maps, core_ids=list(range(NCORES)))

    out = np.empty((B, T, D), np.float32)
    for k in range(NCORES):
        b, j = divmod(k, CPB)
        blocks = np.asarray(res.results[k]["outT"]).reshape(NT, DC, 128, TT)
        for tt in range(NT):
            out[b, j * CT + tt * TT : j * CT + (tt + 1) * TT, :] = blocks[tt].reshape(D, TT).T
    return out
